# version 33
# baseline (speedup 1.0000x reference)
import numpy as np
import concourse.bass as bass
import concourse.bass_utils as _bu
import concourse.mybir as mybir
from concourse.bass_utils import run_bass_kernel_spmd
from contextlib import ExitStack

if not getattr(_bu, "_ldw_opt_shim", False):
    _orig_run_command = _bu.run_command

    def _run_command_ldw(cmd, *args, **kwargs):
        try:
            cmd = ["--enable-ldw-opt=true" if c == "--enable-ldw-opt=false" else c
                   for c in cmd]
        except TypeError:
            pass
        return _orig_run_command(cmd, *args, **kwargs)

    _bu.run_command = _run_command_ldw
    _bu._ldw_opt_shim = True

F32 = mybir.dt.float32
BF16 = mybir.dt.bfloat16
AF = mybir.ActivationFunctionType
ALU = mybir.AluOpType

B_CORE = 2
S = 4096
Dm = 1024
NQT = S // 128
EB = 256
NEB = S // EB
TQ = 4


class Sched:

    def __init__(self):
        self.lists = {"tensor": [], "vector": [], "scalar": [], "gpsimd": [], "sync": []}
        self.ctr = {}

    def cur(self, sem):
        return self.ctr.get(sem, 0)

    def emit(self, eng, fn, waits=(), inc=None, inc_n=1):
        waits = [(s, v) for (s, v) in waits if v > 0]
        if inc is not None:
            self.ctr[inc] = self.ctr.get(inc, 0) + inc_n
        self.lists[eng].append((waits, fn, inc, inc_n))

    def check_deadlock(self):
        idx = {k: 0 for k in self.lists}
        counts = {}
        progress = True
        while progress:
            progress = False
            for eng, lst in self.lists.items():
                while idx[eng] < len(lst):
                    waits, _, inc, inc_n = lst[idx[eng]]
                    if all(counts.get(s, 0) >= v for (s, v) in waits):
                        if inc is not None:
                            counts[inc] = counts.get(inc, 0) + inc_n
                        idx[eng] += 1
                        progress = True
                    else:
                        break
        stuck = {e: (i, len(self.lists[e])) for e, i in idx.items() if i < len(self.lists[e])}
        if stuck:
            details = []
            for e, (i, n) in stuck.items():
                waits = self.lists[e][i][0]
                details.append(f"{e}@{i}/{n} waits {waits} have {[(s, counts.get(s, 0)) for s, _ in waits]}")
            raise RuntimeError("plan deadlock: " + "; ".join(details))


def build_nc():
    nc = bass.Bass("TRN2", target_bir_lowering=False)

    x_nat = nc.dram_tensor("x_nat", [B_CORE, S, Dm], F32, kind="ExternalInput").ap()
    xT = nc.dram_tensor("xT", [B_CORE, Dm, S], F32, kind="ExternalInput").ap()
    wv_d = nc.dram_tensor("wv", [Dm, Dm], F32, kind="ExternalInput").ap()
    woT_d = nc.dram_tensor("woT", [Dm, Dm], F32, kind="ExternalInput").ap()
    w3dT_d = nc.dram_tensor("w3dT", [8, 128, 3], F32, kind="ExternalInput").ap()
    bvT_d = nc.dram_tensor("bvT", [128, 8], F32, kind="ExternalInput").ap()
    bo_d = nc.dram_tensor("bo_row", [1, Dm], F32, kind="ExternalInput").ap()
    b3d_d = nc.dram_tensor("b3d_col", [3, 1], F32, kind="ExternalInput").ap()
    fW_d = nc.dram_tensor("fW", [Dm, Dm], F32, kind="ExternalInput").ap()
    fA1_d = nc.dram_tensor("fA1", [Dm, Dm], F32, kind="ExternalInput").ap()
    fA2_d = nc.dram_tensor("fA2", [Dm, Dm], F32, kind="ExternalInput").ap()
    fbias_d = nc.dram_tensor("fbias", [1, Dm], F32, kind="ExternalInput").ap()
    ident_d = nc.dram_tensor("ident", [128, 128], F32, kind="ExternalInput").ap()
    mask0_d = nc.dram_tensor("mask0", [128, EB], F32, kind="ExternalInput").ap()
    mask1_d = nc.dram_tensor("mask1", [128, EB], F32, kind="ExternalInput").ap()
    ones_d = nc.dram_tensor("ones_in", [128, 128], F32, kind="ExternalInput").ap()
    ones3_d = nc.dram_tensor("ones3_in", [3, 1], F32, kind="ExternalInput").ap()
    onescol_d = nc.dram_tensor("onescol_in", [128, 1], F32, kind="ExternalInput").ap()
    onesrow4k_d = nc.dram_tensor("onesrow4k", [1, S], F32, kind="ExternalInput").ap()
    out_d = nc.dram_tensor("out", [B_CORE, S, Dm], F32, kind="ExternalOutput").ap()
    wvo_scr = nc.dram_tensor("wvo_scr", [128, 8 * Dm], BF16, kind="Internal").ap()

    es = ExitStack()

    def sb(name, shape, dtype):
        return es.enter_context(nc.sbuf_tensor(name, shape, dtype)).ap()

    W_sb = sb("W_sb", [128, 8 * Dm], BF16)
    A1_sb = sb("A1_sb", [128, 8 * Dm], BF16)
    A2_sb = sb("A2_sb", [128, 8 * Dm], BF16)
    u_b = sb("u_b", [128, NQT * Dm], BF16)
    Dt = sb("Dt", [128, 32 * EB], BF16)
    Az = sb("Az", [128, TQ * Dm], F32)
    Bz = sb("Bz", [128, TQ * Dm], F32)
    Cz = sb("Cz", [128, TQ * Dm], BF16)
    D2 = sb("D2", [128, TQ * Dm], BF16)
    aug5a = sb("aug5a", [101, S], BF16)
    aug5b = sb("aug5b", [101, S], BF16)
    CZB = sb("CZB", [128, 2048], BF16)
    Dbnd = sb("Dbnd", [128, 4 * EB], BF16)
    mrow_st = CZB[0:1, 512:1024]
    bvo_bc = sb("bvo_bc", [128, Dm], BF16)
    w3dT_sb = sb("w3dT_sb", [128, 24], BF16)
    bvT_sb = sb("bvT_sb", [128, 8], BF16)
    ident_sb = sb("ident_sb", [128, 128], BF16)
    ones_row = sb("ones_row", [1, 128], BF16)
    ones_rowf = sb("ones_rowf", [1, 128], F32)
    ones3 = sb("ones3", [3, 1], BF16)
    ones_col = sb("ones_col", [128, 1], BF16)
    mask0 = sb("mask0_sb", [128, EB], BF16)
    mask1 = sb("mask1_sb", [128, EB], BF16)
    bias_row = sb("bias_row", [1, Dm], BF16)
    b3d_col = sb("b3d_sb", [3, 1], F32)

    possq = CZB[0:3, 1024:1536]
    stat = sb("stat", [128, 64], F32)
    eps_col = sb("eps_col", [128, 1], F32)

    az_mloc = nc.lookup_mloc(Az.tensor if hasattr(Az, "tensor") else Az)
    bz_mloc = nc.lookup_mloc(Bz.tensor if hasattr(Bz, "tensor") else Bz)
    wv_all = nc.alloc_sbuf_tensor_at("wv_all", [128, 8 * Dm], BF16,
                                     offset=az_mloc.addr).ap()
    woT_all = nc.alloc_sbuf_tensor_at("woT_all", [128, 8 * Dm], BF16,
                                      offset=bz_mloc.addr).ap()
    czb_mloc = nc.lookup_mloc(CZB.tensor if hasattr(CZB, "tensor") else CZB)
    bvo_row = nc.alloc_sbuf_tensor_at("bvo_row", [1, Dm], F32, offset=czb_mloc.addr).ap()
    ps = es.enter_context(nc.psum_tensor("ps", [128, 2560], F32)).ap()
    ps_tr = es.enter_context(nc.psum_tensor("ps_tr", [128, 1024], BF16)).ap()
    ps_f = es.enter_context(nc.psum_tensor("ps_f", [128, 1024], F32)).ap()

    def pbank(b):
        if b <= 4:
            return ps[:, 512 * b:512 * (b + 1)]
        assert b in (6, 7)
        return ps_f[:, 512 * (b - 6):512 * (b - 5)]

    sems = {}
    for name in ["s_t", "s_v", "s_s", "s_ldw", "s_lde", "s_ldo", "s_xc", "s_scr",
                 "s_xa0", "s_xa1", "s_xa2", "s_xa3", "s_ob0", "s_ob1", "s_ob2", "s_ob3", "s_m"]:
        sems[name] = es.enter_context(nc.semaphore(name))

    P = Sched()

    def gdma(out_ap, in_ap, waits=(), sem="s_ldw"):
        P.emit("gpsimd", lambda e, o=out_ap, i=in_ap: e.dma_start(out=o, in_=i),
               waits=waits, inc=sem, inc_n=16)

    def sdma(out_ap, in_ap, waits=(), sem="s_xc"):
        P.emit("sync", lambda e, o=out_ap, i=in_ap: e.dma_start(out=o, in_=i),
               waits=waits, inc=sem, inc_n=16)

    for dst, src in [(W_sb, fW_d), (A1_sb, fA1_d), (A2_sb, fA2_d)]:
        gdma(dst.rearrange("p (a d) -> p a d", a=8), src.rearrange("(a p) d -> p a d", p=128))
    gdma(w3dT_sb.rearrange("p (a c) -> p a c", a=8), w3dT_d.rearrange("a p c -> p a c"))
    gdma(bvT_sb[:], bvT_d)
    gdma(ident_sb[:], ident_d)
    gdma(mask0[:], mask0_d)
    gdma(mask1[:], mask1_d)
    gdma(ones_row[:], ones_d[0:1, :])
    gdma(ones3[:], ones3_d)
    gdma(ones_col[:], onescol_d)
    gdma(bias_row[:], fbias_d)
    for s4 in range(4):
        gdma(aug5a[32 * s4 + 4:32 * s4 + 5, :], onesrow4k_d)
        gdma(aug5b[32 * s4 + 3:32 * s4 + 4, :], onesrow4k_d)
    sdma(ones_rowf[:], ones_d[0:1, :])
    sdma(bvo_row[:], bo_d)
    sdma(b3d_col[:], b3d_d)
    n_sync_setup = P.cur("s_xc")
    setup_ldw = P.cur("s_ldw")
    P.emit("tensor", lambda e: e.nop(), waits=[("s_ldw", setup_ldw)])
    P.emit("vector", lambda e: e.nop(), waits=[("s_ldw", setup_ldw)])

    gdma(wv_all.rearrange("p (a d) -> p a d", a=8), wv_d.rearrange("(a p) d -> p a d", p=128))
    gdma(woT_all.rearrange("p (a d) -> p a d", a=8), woT_d.rearrange("(a p) d -> p a d", p=128))
    wv_ld = P.cur("s_ldw")
    fbanks = [0, 1, 2, 3, 4, 6, 7]
    groups = [(di, h) for di in range(8) for h in range(2)]
    sweeps = [groups[0:7], groups[7:14], groups[14:16]]
    for sw_i, sweep in enumerate(sweeps):
        for mi in range(8):
            for gi, (di, h) in enumerate(sweep):
                w = []
                if gi == 0 and mi == 0:
                    w.append(("s_ldw", wv_ld))
                is_last_g = gi == len(sweep) - 1
                P.emit("tensor",
                       lambda e, b=fbanks[gi], di_=di, h_=h, mi_=mi:
                       e.matmul(pbank(b), wv_all[:, mi_ * 1024 + di_ * 128:mi_ * 1024 + di_ * 128 + 128],
                                woT_all[:, mi_ * 1024 + h_ * 512:mi_ * 1024 + h_ * 512 + 512],
                                start=(mi_ == 0), stop=(mi_ == 7)),
                       waits=w,
                       inc="s_t" if is_last_g else None)
            if sw_i == 2:
                for h in range(2):
                    P.emit("tensor",
                           lambda e, h_=h, mi_=mi:
                           e.matmul(pbank(2 + h_)[0:1, :], bvT_sb[:, mi_:mi_ + 1],
                                    woT_all[:, mi_ * 1024 + h_ * 512:mi_ * 1024 + h_ * 512 + 512],
                                    start=(mi_ == 0), stop=(mi_ == 7)),
                           inc="s_t" if h == 1 else None)
        t_done = P.cur("s_t")
        for gi, (di, h) in enumerate(sweep):
            P.emit("vector",
                   lambda e, b=fbanks[gi], di_=di, h_=h:
                   e.tensor_copy(Dt[:, di_ * 1024 + h_ * 512: di_ * 1024 + h_ * 512 + 512], pbank(b)),
                   waits=[("s_t", t_done)] if gi == 0 else [],
                   inc="s_v" if gi == len(sweep) - 1 else None)
        if sw_i < 2:
            v_done = P.cur("s_v")
            P.emit("tensor", lambda e: e.nop(), waits=[("s_v", v_done)])
    v_done = P.cur("s_v")
    sdma(wvo_scr, Dt[:], waits=[("s_v", v_done)], sem="s_scr")

    t_done = P.cur("s_t")
    for h in range(2):
        P.emit("vector",
               lambda e, h_=h: e.tensor_tensor(bvo_row[:, h_ * 512:h_ * 512 + 512],
                                               pbank(2 + h_)[0:1, :], bvo_row[:, h_ * 512:h_ * 512 + 512], ALU.add),
               waits=[("s_t", t_done), ("s_xc", n_sync_setup)] if h == 0 else [],
               inc="s_v" if h == 1 else None)
    v_done = P.cur("s_v")
    for h in range(2):
        P.emit("tensor",
               lambda e, h_=h: e.matmul(pbank(2 + h_), ones_rowf[:],
                                        bvo_row[:, h_ * 512:h_ * 512 + 512], start=True, stop=True),
               waits=[("s_v", v_done)] if h == 0 else [],
               inc="s_t" if h == 1 else None)
    t_done = P.cur("s_t")
    for h in range(2):
        P.emit("vector",
               lambda e, h_=h: e.tensor_copy(bvo_bc[:, h_ * 512:h_ * 512 + 512], pbank(2 + h_)),
               waits=[("s_t", t_done)] if h == 0 else [],
               inc="s_v" if h == 1 else None)
    P.emit("vector", lambda e: e.memset(eps_col[:], 1e-5), inc="s_v")

    az_guard = [P.cur("s_v")] * 4
    bz_guard = [0, 0, 0, 0]
    cz_guard = 0
    tanh_guard = 0
    bguard = {k: ("s_v", P.cur("s_v")) for k in range(8)}

    for b in range(B_CORE):
        batch_t = P.cur("s_t")
        if b == 1:
            gdma(Dt[:], wvo_scr, waits=[("s_t", batch_t), ("s_scr", 16)], sem="s_ldw")
        stage_guard = {0: P.cur("s_t"), 1: P.cur("s_t")}
        m_marks = {}

        xT_r = xT[b]
        for qb in range(8):
            stage = Cz if qb % 2 == 0 else D2
            psem = "s_lde" if qb % 2 == 0 else "s_ldo"
            gdma(stage.rearrange("p (a c) -> p a c", a=8),
                 xT_r[:, qb * 512:(qb + 1) * 512].rearrange("(a p) c -> p a c", p=128),
                 waits=[("s_t", stage_guard[qb % 2])], sem=psem)
            ld_need = (psem, P.cur(psem))
            for di in range(8):
                P.emit("tensor",
                       lambda e, di_=di, st=stage: e.matmul(
                           pbank(2)[0:3, :], w3dT_sb[:, di_ * 3:di_ * 3 + 3],
                           st[:, di_ * 512:di_ * 512 + 512], start=(di_ == 0), stop=(di_ == 7)),
                       waits=[ld_need, ("s_ldw", P.cur("s_ldw")), bguard[2]] if di == 0 else [],
                       inc="s_t" if di == 7 else None)
            t_done = P.cur("s_t")
            cols = slice(qb * 512, (qb + 1) * 512)
            P.emit("vector", lambda e, c=cols: e.tensor_scalar(
                aug5a[0:3, c], pbank(2)[0:3, :], b3d_col[:, 0:1], None, ALU.add),
                waits=[("s_t", t_done)], inc="s_v")
            v_pos = P.cur("s_v")
            P.emit("vector", lambda e, c=cols: e.tensor_copy(aug5b[0:3, c], aug5a[0:3, c]),
                   waits=[("s_v", v_pos)])
            P.emit("vector", lambda e, c=cols: e.tensor_tensor(possq, aug5a[0:3, c], aug5a[0:3, c], ALU.mult),
                   inc="s_v")
            v_sq = P.cur("s_v")
            bguard[2] = ("s_v", v_sq)
            P.emit("tensor", lambda e: e.matmul(
                pbank(2)[0:1, :], ones3[:], possq, start=True, stop=True),
                waits=[("s_v", v_sq)], inc="s_t")
            t_done = P.cur("s_t")
            P.emit("vector", lambda e: e.tensor_scalar_mul(mrow_st, pbank(2)[0:1, :], -0.5),
                   waits=[("s_t", t_done), ("s_m", P.cur("s_m"))], inc="s_v")
            v_mr = P.cur("s_v")
            bguard[2] = ("s_v", v_mr)
            gdma(aug5a[3:4, cols], mrow_st, waits=[("s_v", v_mr)], sem="s_m")
            gdma(aug5b[4:5, cols], mrow_st, sem="s_m")
            m_marks[qb] = P.cur("s_m")
            for qtl in range(4):
                qt = qb * 4 + qtl
                ub = (0, 1) if qtl % 2 == 0 else (3, 4)
                w0 = [bguard[ub[0]], bguard[ub[1]]]
                for di in range(8):
                    for h in range(2):
                        P.emit("tensor",
                               lambda e, di_=di, h_=h, st=stage, qtl_=qtl, b_=ub: e.matmul(
                                   pbank(b_[h_]), st[:, di_ * 512 + qtl_ * 128:di_ * 512 + qtl_ * 128 + 128],
                                   Dt[:, di_ * 1024 + h_ * 512:di_ * 1024 + h_ * 512 + 512],
                                   start=(di_ == 0), stop=(di_ == 7)),
                               waits=w0 if (di == 0 and h == 0) else [],
                               inc="s_t" if (di == 7 and h == 1) else None)
                t_done = P.cur("s_t")
                for h in range(2):
                    P.emit("vector",
                           lambda e, h_=h, qt_=qt, b_=ub: e.tensor_tensor(
                               u_b[:, qt_ * 1024 + h_ * 512:qt_ * 1024 + h_ * 512 + 512],
                               pbank(b_[h_]), bvo_bc[:, h_ * 512:h_ * 512 + 512], ALU.add),
                           waits=[("s_t", t_done)] if h == 0 else [],
                           inc="s_v" if h == 1 else None)
                bguard[ub[0]] = ("s_v", P.cur("s_v"))
                bguard[ub[1]] = ("s_v", P.cur("s_v"))
            stage_guard[qb % 2] = P.cur("s_t")

        rep_v = P.cur("s_v")
        rep_m = P.cur("s_m")
        for s4 in range(1, 4):
            gdma(aug5a[32 * s4:32 * s4 + 4, :], aug5a[0:4, :],
                 waits=[("s_v", rep_v), ("s_m", rep_m)] if s4 == 1 else [], sem="s_m")
            gdma(aug5b[32 * s4:32 * s4 + 3, :], aug5b[0:3, :], sem="s_m")
            gdma(aug5b[32 * s4 + 4:32 * s4 + 5, :], aug5b[4:5, :], sem="s_m")
        rep_done = P.cur("s_m")
        u_v_done = P.cur("s_v")
        zstt_inc = {}
        zln_inc = {}
        rc_inc = {}
        b_ready = {}
        r_ready = {0: []}
        x_marks = {}
        czb_guard = [P.cur("s_t"), P.cur("s_t")]
        czb_inc = [0, 0]

        def prefetch_x(eb):
            for qtl in range(2):
                qt = 2 * eb + qtl
                tq = qt % TQ
                sdma(Az[:, tq * 1024:tq * 1024 + 1024], x_nat[b, qt * 128:(qt + 1) * 128, :],
                     waits=[("s_v", az_guard[tq])], sem=f"s_xa{tq}")
                x_marks[qt] = (f"s_xa{tq}", P.cur(f"s_xa{tq}"))

        def emit_eprod_mms(bnd_eb, rest_eb, inject=None):
            inject = dict(inject or {})
            n_relu = 0

            def relu_done():
                nonlocal n_relu
                for fn in inject.pop(n_relu, []):
                    fn()
                n_relu += 1

            st = {"bnd_eb": bnd_eb, "rest_eb": rest_eb, "ev": {}}
            if bnd_eb is not None:
                nkt = 2 * (bnd_eb + 1)
                qcols = slice(bnd_eb * EB, (bnd_eb + 1) * EB)
                slot = (bnd_eb % 2) * 2 * EB
                need_blk = max((nkt - 1) // 4, bnd_eb // 2)
                ev = st["ev"]
                P.emit("tensor", lambda e, kt_=nkt - 2, qc=qcols: e.matmul(
                    pbank(1)[:, 0:EB], aug5a[0:5, kt_ * 128:kt_ * 128 + 128],
                    aug5b[0:5, qc],
                    start=True, stop=False),
                    waits=[bguard[1], ("s_m", m_marks[need_blk]), ("s_m", rep_done),
                           ("s_v", u_v_done)])
                P.emit("tensor", lambda e, kt_=nkt - 1, qc=qcols: e.matmul(
                    pbank(1)[:, EB:2 * EB], aug5a[0:5, kt_ * 128:kt_ * 128 + 128],
                    aug5b[0:5, qc],
                    start=False, stop=True),
                    inc="s_t")
                t_done = P.cur("s_t")
                P.emit("vector", lambda e, sl=slot: e.tensor_scalar(
                    Dbnd[:, sl:sl + 2 * EB], pbank(1)[:, 0:2 * EB], -2.0, 0.0, ALU.mult, ALU.max),
                    waits=[("s_t", t_done)], inc="s_v")
                ev[0] = ev[1] = P.cur("s_v")
                bguard[1] = ("s_v", ev[1])
                relu_done()
                relu_done()
            if rest_eb is not None and rest_eb > 0:
                nkt = 2 * (rest_eb + 1)
                nrest = nkt - 2
                qcols = slice(rest_eb * EB, (rest_eb + 1) * EB)
                banks4 = [3, 4, 2, 1]
                evac_marks = {}
                for p in range((nrest + 1) // 2):
                    kt0 = 2 * p
                    two = kt0 + 1 < nrest
                    bank = banks4[p % 4]
                    w = [bguard[bank]]
                    P.emit("tensor", lambda e, kt_=kt0, bk=bank, qc=qcols, tw=two: e.matmul(
                        pbank(bk)[:, 0:EB], aug5a[0:5, kt_ * 128:kt_ * 128 + 128],
                        aug5b[0:5, qc],
                        start=True, stop=not tw),
                        waits=w, inc=None if two else "s_t")
                    if two:
                        P.emit("tensor", lambda e, kt_=kt0 + 1, bk=bank, qc=qcols: e.matmul(
                            pbank(bk)[:, EB:2 * EB], aug5a[0:5, kt_ * 128:kt_ * 128 + 128],
                            aug5b[0:5, qc],
                            start=False, stop=True),
                            inc="s_t")
                    t_done = P.cur("s_t")
                    wid = 2 * EB if two else EB
                    P.emit("vector", lambda e, kt_=kt0, bk=bank, w_=wid: e.tensor_scalar(
                        Dt[:, kt_ * EB:kt_ * EB + w_], pbank(bk)[:, 0:w_], -2.0, 0.0, ALU.mult, ALU.max),
                        waits=[("s_t", t_done)], inc="s_v")
                    mark = P.cur("s_v")
                    evac_marks[kt0] = mark
                    if two:
                        evac_marks[kt0 + 1] = mark
                    bguard[bank] = ("s_v", mark)
                    relu_done()
                    if two:
                        relu_done()
                st["evac_marks"] = evac_marks
                st["nrest"] = nrest
            for k in sorted(inject):
                for fn in inject[k]:
                    fn()
            return st

        def emit_eprod_scal(st):
            bnd_eb, rest_eb = st["bnd_eb"], st["rest_eb"]
            maskv = None
            if bnd_eb is not None:
                slot = (bnd_eb % 2) * 2 * EB
                ev = st["ev"]
                P.emit("scalar", lambda e, sl=slot: e.activation(
                    Dbnd[:, sl:sl + 2 * EB], Dbnd[:, sl:sl + 2 * EB], AF.Sqrt, scale=1.0 / 64.0),
                    waits=[("s_v", ev[1])], inc="s_s")
                sqrtB = P.cur("s_s")
                P.emit("vector", lambda e, sl=slot: e.tensor_tensor(
                    Dbnd[:, sl:sl + EB], Dbnd[:, sl:sl + EB], mask0[:], ALU.add),
                    waits=[("s_s", sqrtB)])
                P.emit("vector", lambda e, sl=slot: e.tensor_tensor(
                    Dbnd[:, sl + EB:sl + 2 * EB], Dbnd[:, sl + EB:sl + 2 * EB], mask1[:], ALU.add),
                    inc="s_v")
                maskv = P.cur("s_v")
            sq_marks = []
            if rest_eb is not None and rest_eb > 0:
                nrest = st["nrest"]
                evac_marks = st["evac_marks"]
                cuts = sorted({max(2, nrest // 4), nrest // 2, nrest}) if nrest > 4 else [nrest]
                lo = 0
                for hi in cuts:
                    P.emit("scalar", lambda e, l=lo, h=hi: e.activation(
                        Dt[:, l * EB:h * EB], Dt[:, l * EB:h * EB], AF.Sqrt, scale=1.0 / 64.0),
                        waits=[("s_v", evac_marks[hi - 1])], inc="s_s")
                    sq_marks.append((lo, hi, P.cur("s_s")))
                    lo = hi
            if rest_eb is not None:
                exp_marks = []
                for (lo, hi, sqm) in sq_marks:
                    P.emit("scalar", lambda e, l=lo, h=hi: e.activation(
                        Dt[:, l * EB:h * EB], Dt[:, l * EB:h * EB], AF.Exp, scale=-1.0),
                        waits=[("s_s", sqm)], inc="s_s")
                    exp_marks.append((lo, hi, P.cur("s_s")))
                r_ready[rest_eb] = exp_marks
            if bnd_eb is not None:
                slot = (bnd_eb % 2) * 2 * EB
                P.emit("scalar", lambda e, sl=slot: e.activation(
                    Dbnd[:, sl:sl + 2 * EB], Dbnd[:, sl:sl + 2 * EB], AF.Exp, scale=-1.0),
                    waits=[("s_v", maskv)], inc="s_s")
                b_ready[bnd_eb] = P.cur("s_s")

        def emit_attn_q(eb, qtl):
            nkt = 2 * (eb + 1)
            slot = (eb % 2) * 2 * EB
            exp_marks = r_ready[eb]
            if True:
                qt = 2 * eb + qtl
                tq = qt % TQ
                yb = (0, 1) if qtl == 0 else (3, 4)
                yfull = ps[:, 512 * yb[0]:512 * yb[0] + 1024]
                x_need = x_marks[qt]
                nb = 1 if qtl == 0 else 2
                kts = list(range(nb)) + list(range(nkt - 2))
                first_waits = [("s_s", b_ready[eb]), bguard[yb[0]], bguard[yb[1]]]
                r_wait = [bguard[2]] if qtl == 0 else []
                rcol = 1024 + qtl
                for i in range(len(kts)):
                    isb = i < nb
                    kt = kts[i]
                    gkt = (nkt - 2 + kt) if isb else kt
                    if isb:
                        lhs = Dbnd[:, slot + kt * EB + qtl * 128:slot + kt * EB + qtl * 128 + 128]
                    else:
                        lhs = Dt[:, kt * EB + qtl * 128:kt * EB + qtl * 128 + 128]
                    w = first_waits if i == 0 else []
                    if not isb:
                        for (lo, hi, em) in exp_marks:
                            if kt == lo:
                                w = w + [("s_s", em)]
                    P.emit("tensor", lambda e, l=lhs, kt_=gkt, fst=(i == 0), lst=(i == len(kts) - 1), yb_=yb: e.matmul(
                        pbank(yb_[0]), l, u_b[:, kt_ * 1024:kt_ * 1024 + 512],
                        start=fst, stop=lst),
                        waits=w)
                    P.emit("tensor", lambda e, l=lhs, kt_=gkt, fst=(i == 0), lst=(i == len(kts) - 1), yb_=yb: e.matmul(
                        pbank(yb_[1]), l, u_b[:, kt_ * 1024 + 512:kt_ * 1024 + 1024],
                        start=fst, stop=lst))
                    P.emit("tensor", lambda e, l=lhs, fst=(i == 0), lst=(i == len(kts) - 1),
                           rc=rcol, q_=qtl: e.matmul(
                        ps[:, rc:rc + 1], l, ones_col[:],
                        start=(fst and q_ == 0), stop=lst),
                        waits=r_wait if i == 0 else [],
                        inc="s_t" if i == len(kts) - 1 else None)
                t_attn = P.cur("s_t")
                st_sl = stat[:, tq * 4:tq * 4 + 4]
                P.emit("vector", lambda e, sl=st_sl, rc=rcol: e.tensor_copy(sl[:, 3:4], ps[:, rc:rc + 1]),
                       waits=[("s_t", t_attn)], inc="s_v")
                rc_inc[qt] = P.cur("s_v")
                bguard[2] = ("s_v", rc_inc[qt])
                P.emit("vector", lambda e, sl=st_sl: e.reciprocal(sl[:, 0:1], sl[:, 3:4]),
                       waits=[("s_v", rc_inc[qt])], inc="s_v")
                v_rec = P.cur("s_v")
                P.emit("vector", lambda e, sl=st_sl, tq_=tq, yf=yfull: e.scalar_tensor_tensor(
                    Az[:, tq_ * 1024:tq_ * 1024 + 1024], yf, sl[:, 0:1],
                    Az[:, tq_ * 1024:tq_ * 1024 + 1024], ALU.mult, ALU.add,
                    accum_out=sl[:, 1:2]),
                    waits=[("s_v", v_rec), x_need], inc="s_v")
                zstt_inc[qt] = P.cur("s_v")
                bguard[yb[0]] = ("s_v", zstt_inc[qt])
                bguard[yb[1]] = ("s_v", zstt_inc[qt])

        ln_sq_mark = {}
        ln_var_mark = {}
        ln_std_mark = {}

        def ln_sq(eb, qtl):
            qt = 2 * eb + qtl
            tq = qt % TQ
            st_sl = stat[:, tq * 4:tq * 4 + 4]
            P.emit("scalar", lambda e, sl=st_sl, tq_=tq: e.activation(
                Bz[:, tq_ * 1024:tq_ * 1024 + 1024], Az[:, tq_ * 1024:tq_ * 1024 + 1024],
                AF.Square, accum_out=sl[:, 2:3]),
                waits=[("s_v", zstt_inc[qt]), (f"s_ob{tq}", bz_guard[tq])], inc="s_s")
            ln_sq_mark[qt] = P.cur("s_s")

        def ln_stats(eb, qtl):
            qt = 2 * eb + qtl
            tq = qt % TQ
            st_sl = stat[:, tq * 4:tq * 4 + 4]
            P.emit("vector", lambda e, sl=st_sl: e.tensor_scalar_mul(sl[:, 0:1], sl[:, 1:2], 1.0 / Dm),
                   waits=[("s_s", ln_sq_mark[qt])], inc="s_v")
            v_mu = P.cur("s_v")
            P.emit("vector", lambda e, sl=st_sl: e.tensor_tensor(sl[:, 1:2], sl[:, 0:1], sl[:, 0:1], ALU.mult),
                   waits=[("s_v", v_mu)], inc="s_v")
            v_mu2 = P.cur("s_v")
            P.emit("vector", lambda e, sl=st_sl: e.scalar_tensor_tensor(
                sl[:, 2:3], sl[:, 2:3], 1.0 / Dm, sl[:, 1:2], ALU.mult, ALU.subtract),
                waits=[("s_v", v_mu2)], inc="s_v")
            ln_var_mark[qt] = P.cur("s_v")

        def ln_sqrt(eb, qtl):
            qt = 2 * eb + qtl
            tq = qt % TQ
            st_sl = stat[:, tq * 4:tq * 4 + 4]
            P.emit("scalar", lambda e, sl=st_sl: e.activation(sl[:, 3:4], sl[:, 2:3], AF.Sqrt, bias=eps_col[:, 0:1]),
                   waits=[("s_v", ln_var_mark[qt])], inc="s_s")
            ln_std_mark[qt] = P.cur("s_s")

        def ln_norm(eb, qtl):
            qt = 2 * eb + qtl
            tq = qt % TQ
            st_sl = stat[:, tq * 4:tq * 4 + 4]
            P.emit("vector", lambda e, sl=st_sl: e.reciprocal(sl[:, 1:2], sl[:, 3:4]),
                   waits=[("s_s", ln_std_mark[qt])], inc="s_v")
            v_rstd = P.cur("s_v")
            P.emit("vector", lambda e, sl=st_sl, tq_=tq, q_=qtl: e.tensor_scalar(
                CZB[:, q_ * 1024:q_ * 1024 + 1024], Az[:, tq_ * 1024:tq_ * 1024 + 1024],
                sl[:, 0:1], sl[:, 1:2], ALU.subtract, ALU.mult),
                waits=[("s_v", v_rstd), ("s_t", czb_guard[qtl])], inc="s_v")
            czb_inc[qtl] = P.cur("s_v")

        def ln_norm_bz(eb, qtl):
            qt = 2 * eb + qtl
            tq = qt % TQ
            st_sl = stat[:, tq * 4:tq * 4 + 4]
            P.emit("vector", lambda e, sl=st_sl, tq_=tq: e.tensor_scalar(
                Bz[:, tq_ * 1024:tq_ * 1024 + 1024], Az[:, tq_ * 1024:tq_ * 1024 + 1024],
                sl[:, 0:1], sl[:, 1:2], ALU.subtract, ALU.mult),
                inc="s_v")
            zln_inc[qt] = P.cur("s_v")

        def emit_tail_pre(tb, qs_list):
            for idx, qtl in enumerate(qs_list):
                src_half = idx
                for ct in range(8):
                    P.emit("tensor", lambda e, ct_=ct, sh=src_half: e.transpose(
                        ps_tr[:, ct_ * 128:ct_ * 128 + 128],
                        CZB[:, sh * 1024 + ct_ * 128:sh * 1024 + ct_ * 128 + 128], ident_sb[:]),
                        waits=([("s_v", czb_inc[src_half]), bguard[5]] if ct == 0 else []),
                        inc="s_t" if ct == 7 else None)
                t_tr = P.cur("s_t")
                czb_guard[src_half] = t_tr
                P.emit("vector", lambda e, qs_=qtl: e.tensor_copy(
                    D2.rearrange("p (a q) -> p a q", a=8)[:, :, qs_ * 128:qs_ * 128 + 128],
                    ps_tr.rearrange("p (a q) -> p a q", a=8)),
                    waits=[("s_t", t_tr)], inc="s_v")
                bguard[5] = ("s_v", P.cur("s_v"))

        def emit_tail(tb):
            nonlocal cz_guard, tanh_guard
            v_start = P.cur("s_v")
            for layer, (wsb, src_, dst) in enumerate([(W_sb, D2, Cz), (A1_sb, Cz, D2)]):
                v_start = P.cur("s_v")
                for eo in range(8):
                    bank = 6 + (eo % 2)
                    for c in range(8):
                        P.emit("tensor", lambda e, w=wsb, s=src_, c_=c, eo_=eo, bk=bank: e.matmul(
                            pbank(bk), w[:, c_ * 1024 + eo_ * 128:c_ * 1024 + eo_ * 128 + 128],
                            s[:, c_ * 512:c_ * 512 + 512], start=(c_ == 0), stop=(c_ == 7)),
                            waits=([("s_v", v_start), bguard[bank], ("s_s", tanh_guard)]
                                   if c == 0 else []),
                            inc="s_t" if c == 7 else None)
                    t_h = P.cur("s_t")
                    P.emit("vector", lambda e, d=dst, eo_=eo, bk=bank: e.tensor_copy(
                        d[:, eo_ * 512:eo_ * 512 + 512], pbank(bk)),
                        waits=[("s_t", t_h)], inc="s_v")
                    bguard[bank] = ("s_v", P.cur("s_v"))
            v_h2 = P.cur("s_v")
            s_f = {}
            for qtl in range(TQ):
                hb = (0, 1) if qtl % 2 == 0 else (6, 7)
                for e2 in range(8):
                    for h in range(2):
                        P.emit("tensor", lambda e, e2_=e2, h_=h, q_=qtl, b_=hb: e.matmul(
                            pbank(b_[h_]), D2[:, e2_ * 512 + q_ * 128:e2_ * 512 + q_ * 128 + 128],
                            A2_sb[:, e2_ * 1024 + h_ * 512:e2_ * 1024 + h_ * 512 + 512],
                            start=(e2_ == 0), stop=False),
                            waits=([("s_v", v_h2), bguard[hb[0]], bguard[hb[1]]]
                                   if (e2 == 0 and h == 0) else []))
                for h in range(2):
                    P.emit("tensor", lambda e, h_=h, b_=hb: e.matmul(
                        pbank(b_[h_]), ones_row[:], bias_row[:, h_ * 512:h_ * 512 + 512],
                        start=False, stop=True),
                        inc="s_t" if h == 1 else None)
                t_h3 = P.cur("s_t")
                for h in range(2):
                    P.emit("scalar", lambda e, h_=h, tq_=qtl, b_=hb: e.activation(
                        Az[:, tq_ * 1024 + h_ * 512:tq_ * 1024 + h_ * 512 + 512], pbank(b_[h_]), AF.Tanh),
                        waits=[("s_t", t_h3)] if h == 0 else [],
                        inc="s_s" if h == 1 else None)
                s_f[qtl] = P.cur("s_s")
                tanh_guard = s_f[qtl]
                bguard[hb[0]] = ("s_s", tanh_guard)
                bguard[hb[1]] = ("s_s", tanh_guard)
            s_sq = {}
            for qtl in range(TQ):
                st_sl = stat[:, (8 + qtl) * 4:(8 + qtl) * 4 + 4]
                P.emit("vector", lambda e, sl=st_sl, tq_=qtl: e.scalar_tensor_tensor(
                    Az[:, tq_ * 1024:tq_ * 1024 + 1024], Az[:, tq_ * 1024:tq_ * 1024 + 1024], 0.0,
                    Bz[:, tq_ * 1024:tq_ * 1024 + 1024], ALU.add, ALU.add, accum_out=sl[:, 0:1]),
                    waits=[("s_s", s_f[qtl])], inc="s_v")
                v_z2 = P.cur("s_v")
                P.emit("scalar", lambda e, sl=st_sl, tq_=qtl: e.activation(
                    Bz[:, tq_ * 1024:tq_ * 1024 + 1024], Az[:, tq_ * 1024:tq_ * 1024 + 1024],
                    AF.Square, accum_out=sl[:, 2:3]),
                    waits=[("s_v", v_z2)], inc="s_s")
                s_sq[qtl] = P.cur("s_s")
            v_var = {}
            for qtl in range(TQ):
                st_sl = stat[:, (8 + qtl) * 4:(8 + qtl) * 4 + 4]
                P.emit("vector", lambda e, sl=st_sl: e.tensor_scalar_mul(sl[:, 0:1], sl[:, 0:1], 1.0 / Dm),
                       waits=[("s_s", s_sq[qtl])], inc="s_v")
                v_mu = P.cur("s_v")
                P.emit("vector", lambda e, sl=st_sl: e.tensor_tensor(sl[:, 1:2], sl[:, 0:1], sl[:, 0:1], ALU.mult),
                       waits=[("s_v", v_mu)], inc="s_v")
                v_mu2 = P.cur("s_v")
                P.emit("vector", lambda e, sl=st_sl: e.scalar_tensor_tensor(
                    sl[:, 2:3], sl[:, 2:3], 1.0 / Dm, sl[:, 1:2], ALU.mult, ALU.subtract),
                    waits=[("s_v", v_mu2)], inc="s_v")
                v_var[qtl] = P.cur("s_v")
            s_std = {}
            for qtl in range(TQ):
                st_sl = stat[:, (8 + qtl) * 4:(8 + qtl) * 4 + 4]
                P.emit("scalar", lambda e, sl=st_sl: e.activation(sl[:, 3:4], sl[:, 2:3], AF.Sqrt, bias=eps_col[:, 0:1]),
                       waits=[("s_v", v_var[qtl])], inc="s_s")
                s_std[qtl] = P.cur("s_s")
            for qtl in range(TQ):
                qt = tb * TQ + qtl
                st_sl = stat[:, (8 + qtl) * 4:(8 + qtl) * 4 + 4]
                P.emit("vector", lambda e, sl=st_sl: e.reciprocal(sl[:, 1:2], sl[:, 3:4]),
                       waits=[("s_s", s_std[qtl])], inc="s_v")
                v_rstd = P.cur("s_v")
                P.emit("vector", lambda e, sl=st_sl, tq_=qtl: e.tensor_scalar(
                    Bz[:, tq_ * 1024:tq_ * 1024 + 1024], Az[:, tq_ * 1024:tq_ * 1024 + 1024],
                    sl[:, 0:1], sl[:, 1:2], ALU.subtract, ALU.mult),
                    waits=[("s_v", v_rstd)], inc="s_v")
                v_out = P.cur("s_v")
                az_guard[qtl] = v_out
                sdma(out_d[b, qt * 128:(qt + 1) * 128, :], Bz[:, qtl * 1024:qtl * 1024 + 1024],
                     waits=[("s_v", v_out)], sem=f"s_ob{qtl}")
                bz_guard[qtl] = P.cur(f"s_ob{qtl}")
            cz_guard = P.cur("s_t")

        prefetch_x(0)
        emit_eprod_scal(emit_eprod_mms(0, None))
        emit_eprod_scal(emit_eprod_mms(1, 0))
        prefetch_x(1)
        for eb in range(NEB):
            emit_attn_q(eb, 0)
            P.emit("scalar", lambda e: e.activation(
                stat[:, 48:49], eps_col[:, 0:1], AF.Sqrt))
            ln_sq(eb, 0)
            emit_attn_q(eb, 1)
            ln_sq(eb, 1)
            inj = {7: [lambda e_=eb: ln_stats(e_, 0), lambda e_=eb: ln_stats(e_, 1),
                       lambda e_=eb: ln_sqrt(e_, 0), lambda e_=eb: ln_sqrt(e_, 1)],
                   9: [lambda e_=eb: ln_norm(e_, 0)],
                   13: [lambda e_=eb: ln_norm(e_, 1)],
                   98: [lambda e_=eb: ln_norm_bz(e_, 0)],
                   99: [lambda e_=eb: ln_norm_bz(e_, 1)]}
            ep_st = emit_eprod_mms(eb + 2 if eb + 2 < NEB else None,
                                   eb + 1 if eb + 1 < NEB else None,
                                   inject=inj)
            emit_eprod_scal(ep_st)
            emit_tail_pre(eb // 2, [0, 1] if eb % 2 == 0 else [2, 3])
            if eb % 2 == 1:
                emit_tail(eb // 2)
                if eb + 1 < NEB:
                    prefetch_x(eb + 1)
                if eb + 2 < NEB:
                    prefetch_x(eb + 2)

    P.emit("sync", lambda e: e.nop(),
           waits=[(f"s_ob{i}", P.cur(f"s_ob{i}")) for i in range(4)])
    P.check_deadlock()

    with nc.Block() as block:
        def run(eng_obj, lst):
            for waits, fn, inc, inc_n in lst:
                for s, v in waits:
                    eng_obj.wait_ge(sems[s], v)
                inst = fn(eng_obj)
                if inc is not None:
                    inst.then_inc(sems[inc], inc_n)

        @block.tensor
        def _(tensor):
            run(tensor, P.lists["tensor"])

        @block.vector
        def _(vector):
            run(vector, P.lists["vector"])

        @block.scalar
        def _(scalar):
            run(scalar, P.lists["scalar"])

        @block.gpsimd
        def _(gpsimd):
            run(gpsimd, P.lists["gpsimd"])

        @block.sync
        def _(sync):
            run(sync, P.lists["sync"])

    es.close()
    return nc


_NC_CACHE = None


def kernel(L0_attn_x, L0_attn_wq, L0_attn_bq, L0_attn_wk, L0_attn_bk, L0_attn_wv,
           L0_attn_bv, L0_attn_wo, L0_attn_bo, L0_attn_w3d, L0_attn_b3d,
           L0_ffn_weight, L0_ffn_aux_weight1, L0_ffn_aux_weight2, L0_ffn_bias):
    global _NC_CACHE
    x = np.ascontiguousarray(np.asarray(L0_attn_x, dtype=np.float32))
    wv = np.ascontiguousarray(np.asarray(L0_attn_wv, dtype=np.float32))
    woT = np.ascontiguousarray(np.asarray(L0_attn_wo, dtype=np.float32).T)
    bv = np.ascontiguousarray(np.asarray(L0_attn_bv, dtype=np.float32))
    bo = np.ascontiguousarray(np.asarray(L0_attn_bo, dtype=np.float32))
    w3d = np.ascontiguousarray(np.asarray(L0_attn_w3d, dtype=np.float32))
    b3d = np.ascontiguousarray(np.asarray(L0_attn_b3d, dtype=np.float32))
    fW = np.ascontiguousarray(np.asarray(L0_ffn_weight, dtype=np.float32))
    fA1 = np.ascontiguousarray(np.asarray(L0_ffn_aux_weight1, dtype=np.float32))
    fA2 = np.ascontiguousarray(np.asarray(L0_ffn_aux_weight2, dtype=np.float32))
    fbias = np.ascontiguousarray(np.asarray(L0_ffn_bias, dtype=np.float32).reshape(1, Dm))

    w3dT = np.ascontiguousarray(w3d.T.reshape(8, 128, 3))
    bvT = np.ascontiguousarray(bv.reshape(8, 128).T)
    bo_row = np.ascontiguousarray(bo.reshape(1, Dm))
    b3d_col = np.ascontiguousarray(b3d.reshape(3, 1))
    ident = np.eye(128, dtype=np.float32)
    ones_in = np.ones((128, 128), dtype=np.float32)
    f = np.arange(EB)[None, :]
    p = np.arange(128)[:, None]
    mask0 = np.where(f >= p, 0.0, 100.0).astype(np.float32)
    mask1 = np.where(f - p >= 128, 0.0, 100.0).astype(np.float32)

    if _NC_CACHE is None:
        _NC_CACHE = build_nc()
    nc = _NC_CACHE

    in_maps = []
    for c in range(8):
        xc = x[2 * c:2 * c + 2]
        in_maps.append({
            "x_nat": np.ascontiguousarray(xc),
            "xT": np.ascontiguousarray(xc.transpose(0, 2, 1)),
            "wv": wv, "woT": woT, "w3dT": w3dT, "bvT": bvT,
            "bo_row": bo_row, "b3d_col": b3d_col,
            "fW": fW, "fA1": fA1, "fA2": fA2, "fbias": fbias,
            "ident": ident, "mask0": mask0, "mask1": mask1, "ones_in": ones_in,
            "ones3_in": np.ones((3, 1), dtype=np.float32),
            "onescol_in": np.ones((128, 1), dtype=np.float32),
            "onesrow4k": np.ones((1, S), dtype=np.float32),
        })
    res = run_bass_kernel_spmd(nc, in_maps, core_ids=list(range(8)))
    out = np.concatenate([np.asarray(r["out"], dtype=np.float32) for r in res.results], axis=0)
    return out.reshape(16, S, Dm)


# version 34 (final)
# speedup vs baseline: 1.0032x; 1.0032x over previous
import numpy as np
import concourse.bass as bass
import concourse.bass_utils as _bu
import concourse.mybir as mybir
from concourse.bass_utils import run_bass_kernel_spmd
from contextlib import ExitStack

if not getattr(_bu, "_ldw_opt_shim", False):
    _orig_run_command = _bu.run_command

    def _run_command_ldw(cmd, *args, **kwargs):
        try:
            cmd = ["--enable-ldw-opt=true" if c == "--enable-ldw-opt=false" else c
                   for c in cmd]
        except TypeError:
            pass
        return _orig_run_command(cmd, *args, **kwargs)

    _bu.run_command = _run_command_ldw
    _bu._ldw_opt_shim = True

F32 = mybir.dt.float32
BF16 = mybir.dt.bfloat16
AF = mybir.ActivationFunctionType
ALU = mybir.AluOpType

B_CORE = 2
S = 4096
Dm = 1024
NQT = S // 128
EB = 256
NEB = S // EB
TQ = 4


class Sched:

    def __init__(self):
        self.lists = {"tensor": [], "vector": [], "scalar": [], "gpsimd": [], "sync": []}
        self.ctr = {}

    def cur(self, sem):
        return self.ctr.get(sem, 0)

    def emit(self, eng, fn, waits=(), inc=None, inc_n=1):
        waits = [(s, v) for (s, v) in waits if v > 0]
        if inc is not None:
            self.ctr[inc] = self.ctr.get(inc, 0) + inc_n
        self.lists[eng].append((waits, fn, inc, inc_n))

    def check_deadlock(self):
        idx = {k: 0 for k in self.lists}
        counts = {}
        progress = True
        while progress:
            progress = False
            for eng, lst in self.lists.items():
                while idx[eng] < len(lst):
                    waits, _, inc, inc_n = lst[idx[eng]]
                    if all(counts.get(s, 0) >= v for (s, v) in waits):
                        if inc is not None:
                            counts[inc] = counts.get(inc, 0) + inc_n
                        idx[eng] += 1
                        progress = True
                    else:
                        break
        stuck = {e: (i, len(self.lists[e])) for e, i in idx.items() if i < len(self.lists[e])}
        if stuck:
            details = []
            for e, (i, n) in stuck.items():
                waits = self.lists[e][i][0]
                details.append(f"{e}@{i}/{n} waits {waits} have {[(s, counts.get(s, 0)) for s, _ in waits]}")
            raise RuntimeError("plan deadlock: " + "; ".join(details))


def build_nc():
    nc = bass.Bass("TRN2", target_bir_lowering=False)

    x_nat = nc.dram_tensor("x_nat", [B_CORE, S, Dm], F32, kind="ExternalInput").ap()
    xT = nc.dram_tensor("xT", [B_CORE, Dm, S], F32, kind="ExternalInput").ap()
    wv_d = nc.dram_tensor("wv", [Dm, Dm], F32, kind="ExternalInput").ap()
    woT_d = nc.dram_tensor("woT", [Dm, Dm], F32, kind="ExternalInput").ap()
    w3dT_d = nc.dram_tensor("w3dT", [8, 128, 3], F32, kind="ExternalInput").ap()
    bvT_d = nc.dram_tensor("bvT", [128, 8], F32, kind="ExternalInput").ap()
    bo_d = nc.dram_tensor("bo_row", [1, Dm], F32, kind="ExternalInput").ap()
    b3d_d = nc.dram_tensor("b3d_col", [3, 1], F32, kind="ExternalInput").ap()
    fW_d = nc.dram_tensor("fW", [Dm, Dm], F32, kind="ExternalInput").ap()
    fA1_d = nc.dram_tensor("fA1", [Dm, Dm], F32, kind="ExternalInput").ap()
    fA2_d = nc.dram_tensor("fA2", [Dm, Dm], F32, kind="ExternalInput").ap()
    fbias_d = nc.dram_tensor("fbias", [1, Dm], F32, kind="ExternalInput").ap()
    ident_d = nc.dram_tensor("ident", [128, 128], F32, kind="ExternalInput").ap()
    mask0_d = nc.dram_tensor("mask0", [128, EB], F32, kind="ExternalInput").ap()
    mask1_d = nc.dram_tensor("mask1", [128, EB], F32, kind="ExternalInput").ap()
    ones_d = nc.dram_tensor("ones_in", [128, 128], F32, kind="ExternalInput").ap()
    ones3_d = nc.dram_tensor("ones3_in", [3, 1], F32, kind="ExternalInput").ap()
    onescol_d = nc.dram_tensor("onescol_in", [128, 1], F32, kind="ExternalInput").ap()
    onesrow4k_d = nc.dram_tensor("onesrow4k", [1, S], F32, kind="ExternalInput").ap()
    out_d = nc.dram_tensor("out", [B_CORE, S, Dm], F32, kind="ExternalOutput").ap()
    wvo_scr = nc.dram_tensor("wvo_scr", [128, 8 * Dm], BF16, kind="Internal").ap()

    es = ExitStack()

    def sb(name, shape, dtype):
        return es.enter_context(nc.sbuf_tensor(name, shape, dtype)).ap()

    W_sb = sb("W_sb", [128, 8 * Dm], BF16)
    A1_sb = sb("A1_sb", [128, 8 * Dm], BF16)
    A2_sb = sb("A2_sb", [128, 8 * Dm], BF16)
    u_b = sb("u_b", [128, NQT * Dm], BF16)
    Dt = sb("Dt", [128, 32 * EB], BF16)
    Az = sb("Az", [128, TQ * Dm], F32)
    Bz = sb("Bz", [128, TQ * Dm], F32)
    Cz = sb("Cz", [128, TQ * Dm], BF16)
    D2 = sb("D2", [128, TQ * Dm], BF16)
    aug5a = sb("aug5a", [101, S], BF16)
    aug5b = sb("aug5b", [101, S], BF16)
    CZB = sb("CZB", [128, 2048], BF16)
    Dbnd = sb("Dbnd", [128, 4 * EB], BF16)
    mrow_st = CZB[0:1, 512:1024]
    bvo_bc = sb("bvo_bc", [128, Dm], BF16)
    w3dT_sb = sb("w3dT_sb", [128, 24], BF16)
    bvT_sb = sb("bvT_sb", [128, 8], BF16)
    ident_sb = sb("ident_sb", [128, 128], BF16)
    ones_row = sb("ones_row", [1, 128], BF16)
    ones_rowf = sb("ones_rowf", [1, 128], F32)
    ones3 = sb("ones3", [3, 1], BF16)
    ones_col = sb("ones_col", [128, 1], BF16)
    mask0 = sb("mask0_sb", [128, EB], BF16)
    mask1 = sb("mask1_sb", [128, EB], BF16)
    bias_row = sb("bias_row", [1, Dm], BF16)
    b3d_col = sb("b3d_sb", [3, 1], F32)

    possq = CZB[0:3, 1024:1536]
    stat = sb("stat", [128, 64], F32)
    eps_col = sb("eps_col", [128, 1], F32)

    az_mloc = nc.lookup_mloc(Az.tensor if hasattr(Az, "tensor") else Az)
    bz_mloc = nc.lookup_mloc(Bz.tensor if hasattr(Bz, "tensor") else Bz)
    wv_all = nc.alloc_sbuf_tensor_at("wv_all", [128, 8 * Dm], BF16,
                                     offset=az_mloc.addr).ap()
    woT_all = nc.alloc_sbuf_tensor_at("woT_all", [128, 8 * Dm], BF16,
                                      offset=bz_mloc.addr).ap()
    czb_mloc = nc.lookup_mloc(CZB.tensor if hasattr(CZB, "tensor") else CZB)
    bvo_row = nc.alloc_sbuf_tensor_at("bvo_row", [1, Dm], F32, offset=czb_mloc.addr).ap()
    ps = es.enter_context(nc.psum_tensor("ps", [128, 2560], F32)).ap()
    ps_tr = es.enter_context(nc.psum_tensor("ps_tr", [128, 1024], BF16)).ap()
    ps_f = es.enter_context(nc.psum_tensor("ps_f", [128, 1024], F32)).ap()

    def pbank(b):
        if b <= 4:
            return ps[:, 512 * b:512 * (b + 1)]
        assert b in (6, 7)
        return ps_f[:, 512 * (b - 6):512 * (b - 5)]

    sems = {}
    for name in ["s_t", "s_v", "s_s", "s_ldw", "s_lde", "s_ldo", "s_xc", "s_scr",
                 "s_xa0", "s_xa1", "s_xa2", "s_xa3", "s_ob0", "s_ob1", "s_ob2", "s_ob3", "s_m"]:
        sems[name] = es.enter_context(nc.semaphore(name))

    P = Sched()

    def gdma(out_ap, in_ap, waits=(), sem="s_ldw"):
        P.emit("gpsimd", lambda e, o=out_ap, i=in_ap: e.dma_start(out=o, in_=i),
               waits=waits, inc=sem, inc_n=16)

    def sdma(out_ap, in_ap, waits=(), sem="s_xc"):
        P.emit("sync", lambda e, o=out_ap, i=in_ap: e.dma_start(out=o, in_=i),
               waits=waits, inc=sem, inc_n=16)

    for dst, src in [(W_sb, fW_d), (A1_sb, fA1_d), (A2_sb, fA2_d)]:
        gdma(dst.rearrange("p (a d) -> p a d", a=8), src.rearrange("(a p) d -> p a d", p=128))
    gdma(w3dT_sb.rearrange("p (a c) -> p a c", a=8), w3dT_d.rearrange("a p c -> p a c"))
    gdma(bvT_sb[:], bvT_d)
    gdma(ident_sb[:], ident_d)
    gdma(mask0[:], mask0_d)
    gdma(mask1[:], mask1_d)
    gdma(ones_row[:], ones_d[0:1, :])
    gdma(ones3[:], ones3_d)
    gdma(ones_col[:], onescol_d)
    gdma(bias_row[:], fbias_d)
    for s4 in range(4):
        gdma(aug5a[32 * s4 + 4:32 * s4 + 5, :], onesrow4k_d)
        gdma(aug5b[32 * s4 + 3:32 * s4 + 4, :], onesrow4k_d)
    sdma(ones_rowf[:], ones_d[0:1, :])
    sdma(bvo_row[:], bo_d)
    sdma(b3d_col[:], b3d_d)
    n_sync_setup = P.cur("s_xc")
    setup_ldw = P.cur("s_ldw")
    P.emit("tensor", lambda e: e.nop(), waits=[("s_ldw", setup_ldw)])
    P.emit("vector", lambda e: e.nop(), waits=[("s_ldw", setup_ldw)])

    gdma(wv_all.rearrange("p (a d) -> p a d", a=8), wv_d.rearrange("(a p) d -> p a d", p=128))
    gdma(woT_all.rearrange("p (a d) -> p a d", a=8), woT_d.rearrange("(a p) d -> p a d", p=128))
    wv_ld = P.cur("s_ldw")
    fbanks = [0, 1, 2, 3, 4, 6, 7]
    groups = [(di, h) for di in range(8) for h in range(2)]
    sweeps = [groups[0:7], groups[7:14], groups[14:16]]
    for sw_i, sweep in enumerate(sweeps):
        for mi in range(8):
            for gi, (di, h) in enumerate(sweep):
                w = []
                if gi == 0 and mi == 0:
                    w.append(("s_ldw", wv_ld))
                is_last_g = gi == len(sweep) - 1
                P.emit("tensor",
                       lambda e, b=fbanks[gi], di_=di, h_=h, mi_=mi:
                       e.matmul(pbank(b), wv_all[:, mi_ * 1024 + di_ * 128:mi_ * 1024 + di_ * 128 + 128],
                                woT_all[:, mi_ * 1024 + h_ * 512:mi_ * 1024 + h_ * 512 + 512],
                                start=(mi_ == 0), stop=(mi_ == 7)),
                       waits=w,
                       inc="s_t" if is_last_g else None)
            if sw_i == 2:
                for h in range(2):
                    P.emit("tensor",
                           lambda e, h_=h, mi_=mi:
                           e.matmul(pbank(2 + h_)[0:1, :], bvT_sb[:, mi_:mi_ + 1],
                                    woT_all[:, mi_ * 1024 + h_ * 512:mi_ * 1024 + h_ * 512 + 512],
                                    start=(mi_ == 0), stop=(mi_ == 7)),
                           inc="s_t" if h == 1 else None)
        t_done = P.cur("s_t")
        for gi, (di, h) in enumerate(sweep):
            P.emit("vector",
                   lambda e, b=fbanks[gi], di_=di, h_=h:
                   e.tensor_copy(Dt[:, di_ * 1024 + h_ * 512: di_ * 1024 + h_ * 512 + 512], pbank(b)),
                   waits=[("s_t", t_done)] if gi == 0 else [],
                   inc="s_v" if gi == len(sweep) - 1 else None)
        if sw_i < 2:
            v_done = P.cur("s_v")
            P.emit("tensor", lambda e: e.nop(), waits=[("s_v", v_done)])
    v_done = P.cur("s_v")
    sdma(wvo_scr, Dt[:], waits=[("s_v", v_done)], sem="s_scr")

    t_done = P.cur("s_t")
    for h in range(2):
        P.emit("vector",
               lambda e, h_=h: e.tensor_tensor(bvo_row[:, h_ * 512:h_ * 512 + 512],
                                               pbank(2 + h_)[0:1, :], bvo_row[:, h_ * 512:h_ * 512 + 512], ALU.add),
               waits=[("s_t", t_done), ("s_xc", n_sync_setup)] if h == 0 else [],
               inc="s_v" if h == 1 else None)
    v_done = P.cur("s_v")
    for h in range(2):
        P.emit("tensor",
               lambda e, h_=h: e.matmul(pbank(2 + h_), ones_rowf[:],
                                        bvo_row[:, h_ * 512:h_ * 512 + 512], start=True, stop=True),
               waits=[("s_v", v_done)] if h == 0 else [],
               inc="s_t" if h == 1 else None)
    t_done = P.cur("s_t")
    for h in range(2):
        P.emit("vector",
               lambda e, h_=h: e.tensor_copy(bvo_bc[:, h_ * 512:h_ * 512 + 512], pbank(2 + h_)),
               waits=[("s_t", t_done)] if h == 0 else [],
               inc="s_v" if h == 1 else None)
    P.emit("vector", lambda e: e.memset(eps_col[:], 1e-5), inc="s_v")

    az_guard = [P.cur("s_v")] * 4
    bz_guard = [0, 0, 0, 0]
    cz_guard = 0
    tanh_guard = 0
    bguard = {k: ("s_v", P.cur("s_v")) for k in range(8)}

    for b in range(B_CORE):
        batch_t = P.cur("s_t")
        if b == 1:
            gdma(Dt[:], wvo_scr, waits=[("s_t", batch_t), ("s_scr", 16)], sem="s_ldw")
        stage_guard = {0: P.cur("s_t"), 1: P.cur("s_t")}
        m_marks = {}

        xT_r = xT[b]
        for qb in range(8):
            stage = Cz if qb % 2 == 0 else D2
            psem = "s_lde" if qb % 2 == 0 else "s_ldo"
            gdma(stage.rearrange("p (a c) -> p a c", a=8),
                 xT_r[:, qb * 512:(qb + 1) * 512].rearrange("(a p) c -> p a c", p=128),
                 waits=[("s_t", stage_guard[qb % 2])], sem=psem)
            ld_need = (psem, P.cur(psem))
            for di in range(8):
                P.emit("tensor",
                       lambda e, di_=di, st=stage: e.matmul(
                           pbank(2)[0:3, :], w3dT_sb[:, di_ * 3:di_ * 3 + 3],
                           st[:, di_ * 512:di_ * 512 + 512], start=(di_ == 0), stop=(di_ == 7)),
                       waits=[ld_need, ("s_ldw", P.cur("s_ldw")), bguard[2]] if di == 0 else [],
                       inc="s_t" if di == 7 else None)
            t_done = P.cur("s_t")
            cols = slice(qb * 512, (qb + 1) * 512)
            P.emit("vector", lambda e, c=cols: e.tensor_scalar(
                aug5a[0:3, c], pbank(2)[0:3, :], b3d_col[:, 0:1], None, ALU.add),
                waits=[("s_t", t_done)], inc="s_v")
            v_pos = P.cur("s_v")
            P.emit("vector", lambda e, c=cols: e.tensor_copy(aug5b[0:3, c], aug5a[0:3, c]),
                   waits=[("s_v", v_pos)])
            P.emit("vector", lambda e, c=cols: e.tensor_tensor(possq, aug5a[0:3, c], aug5a[0:3, c], ALU.mult),
                   inc="s_v")
            v_sq = P.cur("s_v")
            bguard[2] = ("s_v", v_sq)
            P.emit("tensor", lambda e: e.matmul(
                pbank(2)[0:1, :], ones3[:], possq, start=True, stop=True),
                waits=[("s_v", v_sq)], inc="s_t")
            t_done = P.cur("s_t")
            P.emit("vector", lambda e: e.tensor_scalar_mul(mrow_st, pbank(2)[0:1, :], -0.5),
                   waits=[("s_t", t_done), ("s_m", P.cur("s_m"))], inc="s_v")
            v_mr = P.cur("s_v")
            bguard[2] = ("s_v", v_mr)
            gdma(aug5a[3:4, cols], mrow_st, waits=[("s_v", v_mr)], sem="s_m")
            gdma(aug5b[4:5, cols], mrow_st, sem="s_m")
            m_marks[qb] = P.cur("s_m")
            for qtl in range(4):
                qt = qb * 4 + qtl
                ub = (0, 1) if qtl % 2 == 0 else (3, 4)
                w0 = [bguard[ub[0]], bguard[ub[1]]]
                for di in range(8):
                    for h in range(2):
                        P.emit("tensor",
                               lambda e, di_=di, h_=h, st=stage, qtl_=qtl, b_=ub: e.matmul(
                                   pbank(b_[h_]), st[:, di_ * 512 + qtl_ * 128:di_ * 512 + qtl_ * 128 + 128],
                                   Dt[:, di_ * 1024 + h_ * 512:di_ * 1024 + h_ * 512 + 512],
                                   start=(di_ == 0), stop=(di_ == 7)),
                               waits=w0 if (di == 0 and h == 0) else [],
                               inc="s_t" if (di == 7 and h == 1) else None)
                t_done = P.cur("s_t")
                for h in range(2):
                    P.emit("vector",
                           lambda e, h_=h, qt_=qt, b_=ub: e.tensor_tensor(
                               u_b[:, qt_ * 1024 + h_ * 512:qt_ * 1024 + h_ * 512 + 512],
                               pbank(b_[h_]), bvo_bc[:, h_ * 512:h_ * 512 + 512], ALU.add),
                           waits=[("s_t", t_done)] if h == 0 else [],
                           inc="s_v" if h == 1 else None)
                bguard[ub[0]] = ("s_v", P.cur("s_v"))
                bguard[ub[1]] = ("s_v", P.cur("s_v"))
            stage_guard[qb % 2] = P.cur("s_t")

        rep_v = P.cur("s_v")
        rep_m = P.cur("s_m")
        for s4 in range(1, 4):
            gdma(aug5a[32 * s4:32 * s4 + 4, :], aug5a[0:4, :],
                 waits=[("s_v", rep_v), ("s_m", rep_m)] if s4 == 1 else [], sem="s_m")
            gdma(aug5b[32 * s4:32 * s4 + 3, :], aug5b[0:3, :], sem="s_m")
            gdma(aug5b[32 * s4 + 4:32 * s4 + 5, :], aug5b[4:5, :], sem="s_m")
        rep_done = P.cur("s_m")
        u_v_done = P.cur("s_v")
        zstt_inc = {}
        zln_inc = {}
        rc_inc = {}
        b_ready = {}
        r_ready = {0: []}
        x_marks = {}
        czb_guard = [P.cur("s_t"), P.cur("s_t")]
        czb_inc = [0, 0]

        def prefetch_x(eb):
            for qtl in range(2):
                qt = 2 * eb + qtl
                tq = qt % TQ
                sdma(Az[:, tq * 1024:tq * 1024 + 1024], x_nat[b, qt * 128:(qt + 1) * 128, :],
                     waits=[("s_v", az_guard[tq])], sem=f"s_xa{tq}")
                x_marks[qt] = (f"s_xa{tq}", P.cur(f"s_xa{tq}"))

        def emit_eprod_mms(bnd_eb, rest_eb, inject=None):
            inject = dict(inject or {})
            n_relu = 0

            def relu_done():
                nonlocal n_relu
                for fn in inject.pop(n_relu, []):
                    fn()
                n_relu += 1

            st = {"bnd_eb": bnd_eb, "rest_eb": rest_eb, "ev": {}}
            if bnd_eb is not None:
                nkt = 2 * (bnd_eb + 1)
                qcols = slice(bnd_eb * EB, (bnd_eb + 1) * EB)
                slot = (bnd_eb % 2) * 2 * EB
                need_blk = max((nkt - 1) // 4, bnd_eb // 2)
                ev = st["ev"]
                P.emit("tensor", lambda e, kt_=nkt - 2, qc=qcols: e.matmul(
                    pbank(1)[:, 0:EB], aug5a[0:5, kt_ * 128:kt_ * 128 + 128],
                    aug5b[0:5, qc],
                    start=True, stop=False),
                    waits=[bguard[1], ("s_m", m_marks[need_blk]), ("s_m", rep_done),
                           ("s_v", u_v_done)])
                P.emit("tensor", lambda e, kt_=nkt - 1, qc=qcols: e.matmul(
                    pbank(1)[:, EB:2 * EB], aug5a[0:5, kt_ * 128:kt_ * 128 + 128],
                    aug5b[0:5, qc],
                    start=False, stop=True),
                    inc="s_t")
                t_done = P.cur("s_t")
                P.emit("vector", lambda e, sl=slot: e.tensor_scalar(
                    Dbnd[:, sl:sl + 2 * EB], pbank(1)[:, 0:2 * EB], -2.0, 0.0, ALU.mult, ALU.max),
                    waits=[("s_t", t_done)], inc="s_v")
                ev[0] = ev[1] = P.cur("s_v")
                bguard[1] = ("s_v", ev[1])
                relu_done()
                relu_done()
            if rest_eb is not None and rest_eb > 0:
                nkt = 2 * (rest_eb + 1)
                nrest = nkt - 2
                qcols = slice(rest_eb * EB, (rest_eb + 1) * EB)
                banks4 = [3, 4, 2, 1]
                evac_marks = {}
                for p in range((nrest + 1) // 2):
                    kt0 = 2 * p
                    two = kt0 + 1 < nrest
                    bank = banks4[p % 4]
                    w = [bguard[bank]]
                    P.emit("tensor", lambda e, kt_=kt0, bk=bank, qc=qcols, tw=two: e.matmul(
                        pbank(bk)[:, 0:EB], aug5a[0:5, kt_ * 128:kt_ * 128 + 128],
                        aug5b[0:5, qc],
                        start=True, stop=not tw),
                        waits=w, inc=None if two else "s_t")
                    if two:
                        P.emit("tensor", lambda e, kt_=kt0 + 1, bk=bank, qc=qcols: e.matmul(
                            pbank(bk)[:, EB:2 * EB], aug5a[0:5, kt_ * 128:kt_ * 128 + 128],
                            aug5b[0:5, qc],
                            start=False, stop=True),
                            inc="s_t")
                    t_done = P.cur("s_t")
                    wid = 2 * EB if two else EB
                    P.emit("vector", lambda e, kt_=kt0, bk=bank, w_=wid: e.tensor_scalar(
                        Dt[:, kt_ * EB:kt_ * EB + w_], pbank(bk)[:, 0:w_], -2.0, 0.0, ALU.mult, ALU.max),
                        waits=[("s_t", t_done)], inc="s_v")
                    mark = P.cur("s_v")
                    evac_marks[kt0] = mark
                    if two:
                        evac_marks[kt0 + 1] = mark
                    bguard[bank] = ("s_v", mark)
                    relu_done()
                    if two:
                        relu_done()
                st["evac_marks"] = evac_marks
                st["nrest"] = nrest
            for k in sorted(inject):
                for fn in inject[k]:
                    fn()
            return st

        def emit_eprod_scal(st):
            bnd_eb, rest_eb = st["bnd_eb"], st["rest_eb"]
            maskv = None
            if bnd_eb is not None:
                slot = (bnd_eb % 2) * 2 * EB
                ev = st["ev"]
                P.emit("scalar", lambda e, sl=slot: e.activation(
                    Dbnd[:, sl:sl + 2 * EB], Dbnd[:, sl:sl + 2 * EB], AF.Sqrt, scale=1.0 / 64.0),
                    waits=[("s_v", ev[1])], inc="s_s")
                sqrtB = P.cur("s_s")
                P.emit("vector", lambda e, sl=slot: e.tensor_tensor(
                    Dbnd[:, sl:sl + EB], Dbnd[:, sl:sl + EB], mask0[:], ALU.add),
                    waits=[("s_s", sqrtB)])
                P.emit("vector", lambda e, sl=slot: e.tensor_tensor(
                    Dbnd[:, sl + EB:sl + 2 * EB], Dbnd[:, sl + EB:sl + 2 * EB], mask1[:], ALU.add),
                    inc="s_v")
                maskv = P.cur("s_v")
            sq_marks = []
            if rest_eb is not None and rest_eb > 0:
                nrest = st["nrest"]
                evac_marks = st["evac_marks"]
                cuts = sorted({max(2, nrest // 4), nrest // 2, nrest}) if nrest > 4 else [nrest]
                lo = 0
                for hi in cuts:
                    P.emit("scalar", lambda e, l=lo, h=hi: e.activation(
                        Dt[:, l * EB:h * EB], Dt[:, l * EB:h * EB], AF.Sqrt, scale=1.0 / 64.0),
                        waits=[("s_v", evac_marks[hi - 1])], inc="s_s")
                    sq_marks.append((lo, hi, P.cur("s_s")))
                    lo = hi
            if rest_eb is not None:
                exp_marks = []
                for (lo, hi, sqm) in sq_marks:
                    P.emit("scalar", lambda e, l=lo, h=hi: e.activation(
                        Dt[:, l * EB:h * EB], Dt[:, l * EB:h * EB], AF.Exp, scale=-1.0),
                        waits=[("s_s", sqm)], inc="s_s")
                    exp_marks.append((lo, hi, P.cur("s_s")))
                r_ready[rest_eb] = exp_marks
            if bnd_eb is not None:
                slot = (bnd_eb % 2) * 2 * EB
                P.emit("scalar", lambda e, sl=slot: e.activation(
                    Dbnd[:, sl:sl + 2 * EB], Dbnd[:, sl:sl + 2 * EB], AF.Exp, scale=-1.0),
                    waits=[("s_v", maskv)], inc="s_s")
                b_ready[bnd_eb] = P.cur("s_s")

        def emit_attn_q(eb, qtl):
            nkt = 2 * (eb + 1)
            slot = (eb % 2) * 2 * EB
            exp_marks = r_ready[eb]
            if True:
                qt = 2 * eb + qtl
                tq = qt % TQ
                yb = (0, 1) if qtl == 0 else (3, 4)
                yfull = ps[:, 512 * yb[0]:512 * yb[0] + 1024]
                x_need = x_marks[qt]
                nb = 1 if qtl == 0 else 2
                kts = list(range(nb)) + list(range(nkt - 2))
                first_waits = [("s_s", b_ready[eb]), bguard[yb[0]], bguard[yb[1]]]
                r_wait = [bguard[2]] if qtl == 0 else []
                rcol = 1024 + qtl
                for i in range(len(kts)):
                    isb = i < nb
                    kt = kts[i]
                    gkt = (nkt - 2 + kt) if isb else kt
                    if isb:
                        lhs = Dbnd[:, slot + kt * EB + qtl * 128:slot + kt * EB + qtl * 128 + 128]
                    else:
                        lhs = Dt[:, kt * EB + qtl * 128:kt * EB + qtl * 128 + 128]
                    w = first_waits if i == 0 else []
                    if not isb:
                        for (lo, hi, em) in exp_marks:
                            if kt == lo:
                                w = w + [("s_s", em)]
                    P.emit("tensor", lambda e, l=lhs, kt_=gkt, fst=(i == 0), lst=(i == len(kts) - 1), yb_=yb: e.matmul(
                        pbank(yb_[0]), l, u_b[:, kt_ * 1024:kt_ * 1024 + 512],
                        start=fst, stop=lst),
                        waits=w)
                    P.emit("tensor", lambda e, l=lhs, kt_=gkt, fst=(i == 0), lst=(i == len(kts) - 1), yb_=yb: e.matmul(
                        pbank(yb_[1]), l, u_b[:, kt_ * 1024 + 512:kt_ * 1024 + 1024],
                        start=fst, stop=lst))
                    P.emit("tensor", lambda e, l=lhs, fst=(i == 0), lst=(i == len(kts) - 1),
                           rc=rcol, q_=qtl: e.matmul(
                        ps[:, rc:rc + 1], l, ones_col[:],
                        start=(fst and q_ == 0), stop=lst),
                        waits=r_wait if i == 0 else [],
                        inc="s_t" if i == len(kts) - 1 else None)
                t_attn = P.cur("s_t")
                st_sl = stat[:, tq * 4:tq * 4 + 4]
                P.emit("vector", lambda e, sl=st_sl, rc=rcol: e.tensor_copy(sl[:, 3:4], ps[:, rc:rc + 1]),
                       waits=[("s_t", t_attn)], inc="s_v")
                rc_inc[qt] = P.cur("s_v")
                bguard[2] = ("s_v", rc_inc[qt])
                P.emit("vector", lambda e, sl=st_sl: e.reciprocal(sl[:, 0:1], sl[:, 3:4]),
                       waits=[("s_v", rc_inc[qt])], inc="s_v")
                v_rec = P.cur("s_v")
                P.emit("vector", lambda e, sl=st_sl, tq_=tq, yf=yfull: e.scalar_tensor_tensor(
                    Az[:, tq_ * 1024:tq_ * 1024 + 1024], yf, sl[:, 0:1],
                    Az[:, tq_ * 1024:tq_ * 1024 + 1024], ALU.mult, ALU.add,
                    accum_out=sl[:, 1:2]),
                    waits=[("s_v", v_rec), x_need], inc="s_v")
                zstt_inc[qt] = P.cur("s_v")
                bguard[yb[0]] = ("s_v", zstt_inc[qt])
                bguard[yb[1]] = ("s_v", zstt_inc[qt])

        ln_sq_mark = {}
        ln_var_mark = {}
        ln_std_mark = {}

        def ln_sq(eb, qtl):
            qt = 2 * eb + qtl
            tq = qt % TQ
            st_sl = stat[:, tq * 4:tq * 4 + 4]
            P.emit("scalar", lambda e, sl=st_sl, tq_=tq: e.activation(
                Bz[:, tq_ * 1024:tq_ * 1024 + 1024], Az[:, tq_ * 1024:tq_ * 1024 + 1024],
                AF.Square, accum_out=sl[:, 2:3]),
                waits=[("s_v", zstt_inc[qt]), (f"s_ob{tq}", bz_guard[tq])], inc="s_s")
            ln_sq_mark[qt] = P.cur("s_s")

        def ln_stats(eb, qtl):
            qt = 2 * eb + qtl
            tq = qt % TQ
            st_sl = stat[:, tq * 4:tq * 4 + 4]
            P.emit("vector", lambda e, sl=st_sl: e.tensor_scalar_mul(sl[:, 0:1], sl[:, 1:2], 1.0 / Dm),
                   waits=[("s_s", ln_sq_mark[qt])], inc="s_v")
            v_mu = P.cur("s_v")
            P.emit("vector", lambda e, sl=st_sl: e.tensor_tensor(sl[:, 1:2], sl[:, 0:1], sl[:, 0:1], ALU.mult),
                   waits=[("s_v", v_mu)], inc="s_v")
            v_mu2 = P.cur("s_v")
            P.emit("vector", lambda e, sl=st_sl: e.scalar_tensor_tensor(
                sl[:, 2:3], sl[:, 2:3], 1.0 / Dm, sl[:, 1:2], ALU.mult, ALU.subtract),
                waits=[("s_v", v_mu2)], inc="s_v")
            ln_var_mark[qt] = P.cur("s_v")

        def ln_sqrt(eb, qtl):
            qt = 2 * eb + qtl
            tq = qt % TQ
            st_sl = stat[:, tq * 4:tq * 4 + 4]
            P.emit("scalar", lambda e, sl=st_sl: e.activation(sl[:, 3:4], sl[:, 2:3], AF.Sqrt, bias=eps_col[:, 0:1]),
                   waits=[("s_v", ln_var_mark[qt])], inc="s_s")
            ln_std_mark[qt] = P.cur("s_s")

        def ln_norm(eb, qtl):
            qt = 2 * eb + qtl
            tq = qt % TQ
            st_sl = stat[:, tq * 4:tq * 4 + 4]
            P.emit("vector", lambda e, sl=st_sl: e.reciprocal(sl[:, 1:2], sl[:, 3:4]),
                   waits=[("s_s", ln_std_mark[qt])], inc="s_v")
            v_rstd = P.cur("s_v")
            P.emit("vector", lambda e, sl=st_sl, tq_=tq, q_=qtl: e.tensor_scalar(
                CZB[:, q_ * 1024:q_ * 1024 + 1024], Az[:, tq_ * 1024:tq_ * 1024 + 1024],
                sl[:, 0:1], sl[:, 1:2], ALU.subtract, ALU.mult),
                waits=[("s_v", v_rstd), ("s_t", czb_guard[qtl])], inc="s_v")
            czb_inc[qtl] = P.cur("s_v")

        def ln_norm_bz(eb, qtl):
            qt = 2 * eb + qtl
            tq = qt % TQ
            st_sl = stat[:, tq * 4:tq * 4 + 4]
            P.emit("vector", lambda e, sl=st_sl, tq_=tq: e.tensor_scalar(
                Bz[:, tq_ * 1024:tq_ * 1024 + 1024], Az[:, tq_ * 1024:tq_ * 1024 + 1024],
                sl[:, 0:1], sl[:, 1:2], ALU.subtract, ALU.mult),
                inc="s_v")
            zln_inc[qt] = P.cur("s_v")

        def emit_tail_pre(tb, qs_list):
            for idx, qtl in enumerate(qs_list):
                src_half = idx
                for ct in range(8):
                    P.emit("tensor", lambda e, ct_=ct, sh=src_half: e.transpose(
                        ps_tr[:, ct_ * 128:ct_ * 128 + 128],
                        CZB[:, sh * 1024 + ct_ * 128:sh * 1024 + ct_ * 128 + 128], ident_sb[:]),
                        waits=([("s_v", czb_inc[src_half]), bguard[5]] if ct == 0 else []),
                        inc="s_t" if ct == 7 else None)
                t_tr = P.cur("s_t")
                czb_guard[src_half] = t_tr
                P.emit("vector", lambda e, qs_=qtl: e.tensor_copy(
                    D2.rearrange("p (a q) -> p a q", a=8)[:, :, qs_ * 128:qs_ * 128 + 128],
                    ps_tr.rearrange("p (a q) -> p a q", a=8)),
                    waits=[("s_t", t_tr)], inc="s_v")
                bguard[5] = ("s_v", P.cur("s_v"))

        def emit_tail(tb):
            nonlocal cz_guard, tanh_guard
            v_start = P.cur("s_v")
            for layer, (wsb, src_, dst) in enumerate([(W_sb, D2, Cz), (A1_sb, Cz, D2)]):
                v_start = P.cur("s_v")
                for eo in range(8):
                    bank = 6 + (eo % 2)
                    for c in range(8):
                        P.emit("tensor", lambda e, w=wsb, s=src_, c_=c, eo_=eo, bk=bank: e.matmul(
                            pbank(bk), w[:, c_ * 1024 + eo_ * 128:c_ * 1024 + eo_ * 128 + 128],
                            s[:, c_ * 512:c_ * 512 + 512], start=(c_ == 0), stop=(c_ == 7)),
                            waits=([("s_v", v_start), bguard[bank], ("s_s", tanh_guard)]
                                   if c == 0 else []),
                            inc="s_t" if c == 7 else None)
                    t_h = P.cur("s_t")
                    P.emit("vector", lambda e, d=dst, eo_=eo, bk=bank: e.tensor_copy(
                        d[:, eo_ * 512:eo_ * 512 + 512], pbank(bk)),
                        waits=[("s_t", t_h)], inc="s_v")
                    bguard[bank] = ("s_v", P.cur("s_v"))
            v_h2 = P.cur("s_v")
            s_f = {}
            for qtl in range(TQ):
                hb = (0, 1) if qtl % 2 == 0 else (6, 7)
                for e2 in range(8):
                    for h in range(2):
                        P.emit("tensor", lambda e, e2_=e2, h_=h, q_=qtl, b_=hb: e.matmul(
                            pbank(b_[h_]), D2[:, e2_ * 512 + q_ * 128:e2_ * 512 + q_ * 128 + 128],
                            A2_sb[:, e2_ * 1024 + h_ * 512:e2_ * 1024 + h_ * 512 + 512],
                            start=(e2_ == 0), stop=False),
                            waits=([("s_v", v_h2), bguard[hb[0]], bguard[hb[1]]]
                                   if (e2 == 0 and h == 0) else []))
                for h in range(2):
                    P.emit("tensor", lambda e, h_=h, b_=hb: e.matmul(
                        pbank(b_[h_]), ones_row[:], bias_row[:, h_ * 512:h_ * 512 + 512],
                        start=False, stop=True),
                        inc="s_t" if h == 1 else None)
                t_h3 = P.cur("s_t")
                for h in range(2):
                    P.emit("scalar", lambda e, h_=h, tq_=qtl, b_=hb: e.activation(
                        Az[:, tq_ * 1024 + h_ * 512:tq_ * 1024 + h_ * 512 + 512], pbank(b_[h_]), AF.Tanh),
                        waits=[("s_t", t_h3)] if h == 0 else [],
                        inc="s_s" if h == 1 else None)
                s_f[qtl] = P.cur("s_s")
                tanh_guard = s_f[qtl]
                bguard[hb[0]] = ("s_s", tanh_guard)
                bguard[hb[1]] = ("s_s", tanh_guard)
            s_sq = {}
            for qtl in range(TQ):
                st_sl = stat[:, (8 + qtl) * 4:(8 + qtl) * 4 + 4]
                P.emit("vector", lambda e, sl=st_sl, tq_=qtl: e.scalar_tensor_tensor(
                    Az[:, tq_ * 1024:tq_ * 1024 + 1024], Az[:, tq_ * 1024:tq_ * 1024 + 1024], 0.0,
                    Bz[:, tq_ * 1024:tq_ * 1024 + 1024], ALU.add, ALU.add, accum_out=sl[:, 0:1]),
                    waits=[("s_s", s_f[qtl])], inc="s_v")
                v_z2 = P.cur("s_v")
                P.emit("scalar", lambda e, sl=st_sl, tq_=qtl: e.activation(
                    Bz[:, tq_ * 1024:tq_ * 1024 + 1024], Az[:, tq_ * 1024:tq_ * 1024 + 1024],
                    AF.Square, accum_out=sl[:, 2:3]),
                    waits=[("s_v", v_z2)], inc="s_s")
                s_sq[qtl] = P.cur("s_s")
            v_var = {}
            for qtl in range(TQ):
                st_sl = stat[:, (8 + qtl) * 4:(8 + qtl) * 4 + 4]
                P.emit("vector", lambda e, sl=st_sl: e.tensor_scalar_mul(sl[:, 0:1], sl[:, 0:1], 1.0 / Dm),
                       waits=[("s_s", s_sq[qtl])], inc="s_v")
                v_mu = P.cur("s_v")
                P.emit("vector", lambda e, sl=st_sl: e.tensor_tensor(sl[:, 1:2], sl[:, 0:1], sl[:, 0:1], ALU.mult),
                       waits=[("s_v", v_mu)], inc="s_v")
                v_mu2 = P.cur("s_v")
                P.emit("vector", lambda e, sl=st_sl: e.scalar_tensor_tensor(
                    sl[:, 2:3], sl[:, 2:3], 1.0 / Dm, sl[:, 1:2], ALU.mult, ALU.subtract),
                    waits=[("s_v", v_mu2)], inc="s_v")
                v_var[qtl] = P.cur("s_v")
            s_std = {}
            for qtl in range(TQ):
                st_sl = stat[:, (8 + qtl) * 4:(8 + qtl) * 4 + 4]
                P.emit("scalar", lambda e, sl=st_sl: e.activation(sl[:, 3:4], sl[:, 2:3], AF.Sqrt, bias=eps_col[:, 0:1]),
                       waits=[("s_v", v_var[qtl])], inc="s_s")
                s_std[qtl] = P.cur("s_s")
            for qtl in range(TQ):
                qt = tb * TQ + qtl
                st_sl = stat[:, (8 + qtl) * 4:(8 + qtl) * 4 + 4]
                P.emit("vector", lambda e, sl=st_sl: e.reciprocal(sl[:, 1:2], sl[:, 3:4]),
                       waits=[("s_s", s_std[qtl])], inc="s_v")
                v_rstd = P.cur("s_v")
                P.emit("vector", lambda e, sl=st_sl, tq_=qtl: e.tensor_scalar(
                    Bz[:, tq_ * 1024:tq_ * 1024 + 1024], Az[:, tq_ * 1024:tq_ * 1024 + 1024],
                    sl[:, 0:1], sl[:, 1:2], ALU.subtract, ALU.mult),
                    waits=[("s_v", v_rstd)], inc="s_v")
                v_out = P.cur("s_v")
                az_guard[qtl] = v_out
                sdma(out_d[b, qt * 128:(qt + 1) * 128, :], Bz[:, qtl * 1024:qtl * 1024 + 1024],
                     waits=[("s_v", v_out)], sem=f"s_ob{qtl}")
                bz_guard[qtl] = P.cur(f"s_ob{qtl}")
            cz_guard = P.cur("s_t")

        prefetch_x(0)
        emit_eprod_scal(emit_eprod_mms(0, None))
        emit_eprod_scal(emit_eprod_mms(1, 0))
        prefetch_x(1)
        for eb in range(NEB):
            emit_attn_q(eb, 0)
            P.emit("scalar", lambda e: e.activation(
                stat[:, 48:49], eps_col[:, 0:1], AF.Sqrt))
            ln_sq(eb, 0)
            emit_attn_q(eb, 1)
            ln_sq(eb, 1)
            inj = {9: [lambda e_=eb: ln_stats(e_, 0), lambda e_=eb: ln_stats(e_, 1),
                       lambda e_=eb: ln_sqrt(e_, 0), lambda e_=eb: ln_sqrt(e_, 1)],
                   11: [lambda e_=eb: ln_norm(e_, 0)],
                   13: [lambda e_=eb: ln_norm(e_, 1)],
                   98: [lambda e_=eb: ln_norm_bz(e_, 0)],
                   99: [lambda e_=eb: ln_norm_bz(e_, 1)]}
            ep_st = emit_eprod_mms(eb + 2 if eb + 2 < NEB else None,
                                   eb + 1 if eb + 1 < NEB else None,
                                   inject=inj)
            emit_eprod_scal(ep_st)
            emit_tail_pre(eb // 2, [0, 1] if eb % 2 == 0 else [2, 3])
            if eb % 2 == 1:
                emit_tail(eb // 2)
                if eb + 1 < NEB:
                    prefetch_x(eb + 1)
                if eb + 2 < NEB:
                    prefetch_x(eb + 2)

    P.emit("sync", lambda e: e.nop(),
           waits=[(f"s_ob{i}", P.cur(f"s_ob{i}")) for i in range(4)])
    P.check_deadlock()

    with nc.Block() as block:
        def run(eng_obj, lst):
            for waits, fn, inc, inc_n in lst:
                for s, v in waits:
                    eng_obj.wait_ge(sems[s], v)
                inst = fn(eng_obj)
                if inc is not None:
                    inst.then_inc(sems[inc], inc_n)

        @block.tensor
        def _(tensor):
            run(tensor, P.lists["tensor"])

        @block.vector
        def _(vector):
            run(vector, P.lists["vector"])

        @block.scalar
        def _(scalar):
            run(scalar, P.lists["scalar"])

        @block.gpsimd
        def _(gpsimd):
            run(gpsimd, P.lists["gpsimd"])

        @block.sync
        def _(sync):
            run(sync, P.lists["sync"])

    es.close()
    return nc


_NC_CACHE = None


def kernel(L0_attn_x, L0_attn_wq, L0_attn_bq, L0_attn_wk, L0_attn_bk, L0_attn_wv,
           L0_attn_bv, L0_attn_wo, L0_attn_bo, L0_attn_w3d, L0_attn_b3d,
           L0_ffn_weight, L0_ffn_aux_weight1, L0_ffn_aux_weight2, L0_ffn_bias):
    global _NC_CACHE
    x = np.ascontiguousarray(np.asarray(L0_attn_x, dtype=np.float32))
    wv = np.ascontiguousarray(np.asarray(L0_attn_wv, dtype=np.float32))
    woT = np.ascontiguousarray(np.asarray(L0_attn_wo, dtype=np.float32).T)
    bv = np.ascontiguousarray(np.asarray(L0_attn_bv, dtype=np.float32))
    bo = np.ascontiguousarray(np.asarray(L0_attn_bo, dtype=np.float32))
    w3d = np.ascontiguousarray(np.asarray(L0_attn_w3d, dtype=np.float32))
    b3d = np.ascontiguousarray(np.asarray(L0_attn_b3d, dtype=np.float32))
    fW = np.ascontiguousarray(np.asarray(L0_ffn_weight, dtype=np.float32))
    fA1 = np.ascontiguousarray(np.asarray(L0_ffn_aux_weight1, dtype=np.float32))
    fA2 = np.ascontiguousarray(np.asarray(L0_ffn_aux_weight2, dtype=np.float32))
    fbias = np.ascontiguousarray(np.asarray(L0_ffn_bias, dtype=np.float32).reshape(1, Dm))

    w3dT = np.ascontiguousarray(w3d.T.reshape(8, 128, 3))
    bvT = np.ascontiguousarray(bv.reshape(8, 128).T)
    bo_row = np.ascontiguousarray(bo.reshape(1, Dm))
    b3d_col = np.ascontiguousarray(b3d.reshape(3, 1))
    ident = np.eye(128, dtype=np.float32)
    ones_in = np.ones((128, 128), dtype=np.float32)
    f = np.arange(EB)[None, :]
    p = np.arange(128)[:, None]
    mask0 = np.where(f >= p, 0.0, 100.0).astype(np.float32)
    mask1 = np.where(f - p >= 128, 0.0, 100.0).astype(np.float32)

    if _NC_CACHE is None:
        _NC_CACHE = build_nc()
    nc = _NC_CACHE

    in_maps = []
    for c in range(8):
        xc = x[2 * c:2 * c + 2]
        in_maps.append({
            "x_nat": np.ascontiguousarray(xc),
            "xT": np.ascontiguousarray(xc.transpose(0, 2, 1)),
            "wv": wv, "woT": woT, "w3dT": w3dT, "bvT": bvT,
            "bo_row": bo_row, "b3d_col": b3d_col,
            "fW": fW, "fA1": fA1, "fA2": fA2, "fbias": fbias,
            "ident": ident, "mask0": mask0, "mask1": mask1, "ones_in": ones_in,
            "ones3_in": np.ones((3, 1), dtype=np.float32),
            "onescol_in": np.ones((128, 1), dtype=np.float32),
            "onesrow4k": np.ones((1, S), dtype=np.float32),
        })
    res = run_bass_kernel_spmd(nc, in_maps, core_ids=list(range(8)))
    out = np.concatenate([np.asarray(r["out"], dtype=np.float32) for r in res.results], axis=0)
    return out.reshape(16, S, Dm)


# version 35
# speedup vs baseline: 1.0115x; 1.0083x over previous
import numpy as np
import concourse.bass as bass
import concourse.bass_utils as _bu
import concourse.mybir as mybir
from concourse.bass_utils import run_bass_kernel_spmd
from contextlib import ExitStack

if not getattr(_bu, "_ldw_opt_shim", False):
    _orig_run_command = _bu.run_command

    def _run_command_ldw(cmd, *args, **kwargs):
        try:
            cmd = ["--enable-ldw-opt=true" if c == "--enable-ldw-opt=false" else c
                   for c in cmd]
        except TypeError:
            pass
        return _orig_run_command(cmd, *args, **kwargs)

    _bu.run_command = _run_command_ldw
    _bu._ldw_opt_shim = True

F32 = mybir.dt.float32
BF16 = mybir.dt.bfloat16
AF = mybir.ActivationFunctionType
ALU = mybir.AluOpType

B_CORE = 2
S = 4096
Dm = 1024
NQT = S // 128
EB = 256
NEB = S // EB
TQ = 4


class Sched:

    def __init__(self):
        self.lists = {"tensor": [], "vector": [], "scalar": [], "gpsimd": [], "sync": []}
        self.ctr = {}

    def cur(self, sem):
        return self.ctr.get(sem, 0)

    def emit(self, eng, fn, waits=(), inc=None, inc_n=1):
        waits = [(s, v) for (s, v) in waits if v > 0]
        if inc is not None:
            self.ctr[inc] = self.ctr.get(inc, 0) + inc_n
        self.lists[eng].append((waits, fn, inc, inc_n))

    def check_deadlock(self):
        idx = {k: 0 for k in self.lists}
        counts = {}
        progress = True
        while progress:
            progress = False
            for eng, lst in self.lists.items():
                while idx[eng] < len(lst):
                    waits, _, inc, inc_n = lst[idx[eng]]
                    if all(counts.get(s, 0) >= v for (s, v) in waits):
                        if inc is not None:
                            counts[inc] = counts.get(inc, 0) + inc_n
                        idx[eng] += 1
                        progress = True
                    else:
                        break
        stuck = {e: (i, len(self.lists[e])) for e, i in idx.items() if i < len(self.lists[e])}
        if stuck:
            details = []
            for e, (i, n) in stuck.items():
                waits = self.lists[e][i][0]
                details.append(f"{e}@{i}/{n} waits {waits} have {[(s, counts.get(s, 0)) for s, _ in waits]}")
            raise RuntimeError("plan deadlock: " + "; ".join(details))


def build_nc():
    nc = bass.Bass("TRN2", target_bir_lowering=False)

    x_nat = nc.dram_tensor("x_nat", [B_CORE, S, Dm], F32, kind="ExternalInput").ap()
    xT = nc.dram_tensor("xT", [B_CORE, Dm, S], F32, kind="ExternalInput").ap()
    wv_d = nc.dram_tensor("wv", [Dm, Dm], F32, kind="ExternalInput").ap()
    woT_d = nc.dram_tensor("woT", [Dm, Dm], F32, kind="ExternalInput").ap()
    w3dT_d = nc.dram_tensor("w3dT", [8, 128, 3], F32, kind="ExternalInput").ap()
    bvT_d = nc.dram_tensor("bvT", [128, 8], F32, kind="ExternalInput").ap()
    bo_d = nc.dram_tensor("bo_row", [1, Dm], F32, kind="ExternalInput").ap()
    b3d_d = nc.dram_tensor("b3d_col", [3, 1], F32, kind="ExternalInput").ap()
    fW_d = nc.dram_tensor("fW", [Dm, Dm], F32, kind="ExternalInput").ap()
    fA1_d = nc.dram_tensor("fA1", [Dm, Dm], F32, kind="ExternalInput").ap()
    fA2_d = nc.dram_tensor("fA2", [Dm, Dm], F32, kind="ExternalInput").ap()
    fbias_d = nc.dram_tensor("fbias", [1, Dm], F32, kind="ExternalInput").ap()
    ident_d = nc.dram_tensor("ident", [128, 128], F32, kind="ExternalInput").ap()
    mask0_d = nc.dram_tensor("mask0", [128, EB], F32, kind="ExternalInput").ap()
    mask1_d = nc.dram_tensor("mask1", [128, EB], F32, kind="ExternalInput").ap()
    ones_d = nc.dram_tensor("ones_in", [128, 128], F32, kind="ExternalInput").ap()
    ones3_d = nc.dram_tensor("ones3_in", [3, 1], F32, kind="ExternalInput").ap()
    onescol_d = nc.dram_tensor("onescol_in", [128, 1], F32, kind="ExternalInput").ap()
    onesrow4k_d = nc.dram_tensor("onesrow4k", [1, S], F32, kind="ExternalInput").ap()
    out_d = nc.dram_tensor("out", [B_CORE, S, Dm], F32, kind="ExternalOutput").ap()
    wvo_scr = nc.dram_tensor("wvo_scr", [128, 8 * Dm], BF16, kind="Internal").ap()

    es = ExitStack()

    def sb(name, shape, dtype):
        return es.enter_context(nc.sbuf_tensor(name, shape, dtype)).ap()

    W_sb = sb("W_sb", [128, 8 * Dm], BF16)
    A1_sb = sb("A1_sb", [128, 8 * Dm], BF16)
    A2_sb = sb("A2_sb", [128, 8 * Dm], BF16)
    u_b = sb("u_b", [128, NQT * Dm], BF16)
    Dt = sb("Dt", [128, 32 * EB], BF16)
    Az = sb("Az", [128, TQ * Dm], F32)
    Bz = sb("Bz", [128, TQ * Dm], F32)
    Cz = sb("Cz", [128, TQ * Dm], BF16)
    D2 = sb("D2", [128, TQ * Dm], BF16)
    aug5a = sb("aug5a", [101, S], BF16)
    aug5b = sb("aug5b", [101, S], BF16)
    CZB = sb("CZB", [128, 2048], BF16)
    Dbnd = sb("Dbnd", [128, 4 * EB], BF16)
    mrow_st = CZB[0:1, 512:1024]
    bvo_bc = sb("bvo_bc", [128, Dm], BF16)
    w3dT_sb = sb("w3dT_sb", [128, 24], BF16)
    bvT_sb = sb("bvT_sb", [128, 8], BF16)
    ident_sb = sb("ident_sb", [128, 128], BF16)
    ones_row = sb("ones_row", [1, 128], BF16)
    ones_rowf = sb("ones_rowf", [1, 128], F32)
    ones3 = sb("ones3", [3, 1], BF16)
    ones_col = sb("ones_col", [128, 1], BF16)
    mask0 = sb("mask0_sb", [128, EB], BF16)
    mask1 = sb("mask1_sb", [128, EB], BF16)
    bias_row = sb("bias_row", [1, Dm], BF16)
    b3d_col = sb("b3d_sb", [3, 1], F32)

    possq = CZB[0:3, 1024:1536]
    stat = sb("stat", [128, 64], F32)
    eps_col = sb("eps_col", [128, 1], F32)

    az_mloc = nc.lookup_mloc(Az.tensor if hasattr(Az, "tensor") else Az)
    bz_mloc = nc.lookup_mloc(Bz.tensor if hasattr(Bz, "tensor") else Bz)
    wv_all = nc.alloc_sbuf_tensor_at("wv_all", [128, 8 * Dm], BF16,
                                     offset=az_mloc.addr).ap()
    woT_all = nc.alloc_sbuf_tensor_at("woT_all", [128, 8 * Dm], BF16,
                                      offset=bz_mloc.addr).ap()
    czb_mloc = nc.lookup_mloc(CZB.tensor if hasattr(CZB, "tensor") else CZB)
    bvo_row = nc.alloc_sbuf_tensor_at("bvo_row", [1, Dm], F32, offset=czb_mloc.addr).ap()
    ps = es.enter_context(nc.psum_tensor("ps", [128, 2560], F32)).ap()
    ps_tr = es.enter_context(nc.psum_tensor("ps_tr", [128, 1024], BF16)).ap()
    ps_f = es.enter_context(nc.psum_tensor("ps_f", [128, 1024], F32)).ap()

    def pbank(b):
        if b <= 4:
            return ps[:, 512 * b:512 * (b + 1)]
        assert b in (6, 7)
        return ps_f[:, 512 * (b - 6):512 * (b - 5)]

    sems = {}
    for name in ["s_t", "s_v", "s_s", "s_ldw", "s_lde", "s_ldo", "s_xc", "s_scr",
                 "s_xa0", "s_xa1", "s_xa2", "s_xa3", "s_ob0", "s_ob1", "s_ob2", "s_ob3", "s_m"]:
        sems[name] = es.enter_context(nc.semaphore(name))

    P = Sched()

    def gdma(out_ap, in_ap, waits=(), sem="s_ldw"):
        P.emit("gpsimd", lambda e, o=out_ap, i=in_ap: e.dma_start(out=o, in_=i),
               waits=waits, inc=sem, inc_n=16)

    def sdma(out_ap, in_ap, waits=(), sem="s_xc"):
        P.emit("sync", lambda e, o=out_ap, i=in_ap: e.dma_start(out=o, in_=i),
               waits=waits, inc=sem, inc_n=16)

    for dst, src in [(W_sb, fW_d), (A1_sb, fA1_d), (A2_sb, fA2_d)]:
        gdma(dst.rearrange("p (a d) -> p a d", a=8), src.rearrange("(a p) d -> p a d", p=128))
    gdma(w3dT_sb.rearrange("p (a c) -> p a c", a=8), w3dT_d.rearrange("a p c -> p a c"))
    gdma(bvT_sb[:], bvT_d)
    gdma(ident_sb[:], ident_d)
    gdma(mask0[:], mask0_d)
    gdma(mask1[:], mask1_d)
    gdma(ones_row[:], ones_d[0:1, :])
    gdma(ones3[:], ones3_d)
    gdma(ones_col[:], onescol_d)
    gdma(bias_row[:], fbias_d)
    for s4 in range(4):
        gdma(aug5a[32 * s4 + 4:32 * s4 + 5, :], onesrow4k_d)
        gdma(aug5b[32 * s4 + 3:32 * s4 + 4, :], onesrow4k_d)
    sdma(ones_rowf[:], ones_d[0:1, :])
    sdma(bvo_row[:], bo_d)
    sdma(b3d_col[:], b3d_d)
    n_sync_setup = P.cur("s_xc")
    setup_ldw = P.cur("s_ldw")
    P.emit("tensor", lambda e: e.nop(), waits=[("s_ldw", setup_ldw)])
    P.emit("vector", lambda e: e.nop(), waits=[("s_ldw", setup_ldw)])

    gdma(wv_all.rearrange("p (a d) -> p a d", a=8), wv_d.rearrange("(a p) d -> p a d", p=128))
    gdma(woT_all.rearrange("p (a d) -> p a d", a=8), woT_d.rearrange("(a p) d -> p a d", p=128))
    wv_ld = P.cur("s_ldw")
    fbanks = [0, 1, 2, 3, 4, 6, 7]
    groups = [(di, h) for di in range(8) for h in range(2)]
    sweeps = [groups[0:7], groups[7:14], groups[14:16]]
    for sw_i, sweep in enumerate(sweeps):
        for mi in range(8):
            for gi, (di, h) in enumerate(sweep):
                w = []
                if gi == 0 and mi == 0:
                    w.append(("s_ldw", wv_ld))
                is_last_g = gi == len(sweep) - 1
                P.emit("tensor",
                       lambda e, b=fbanks[gi], di_=di, h_=h, mi_=mi:
                       e.matmul(pbank(b), wv_all[:, mi_ * 1024 + di_ * 128:mi_ * 1024 + di_ * 128 + 128],
                                woT_all[:, mi_ * 1024 + h_ * 512:mi_ * 1024 + h_ * 512 + 512],
                                start=(mi_ == 0), stop=(mi_ == 7)),
                       waits=w,
                       inc="s_t" if is_last_g else None)
            if sw_i == 2:
                for h in range(2):
                    P.emit("tensor",
                           lambda e, h_=h, mi_=mi:
                           e.matmul(pbank(2 + h_)[0:1, :], bvT_sb[:, mi_:mi_ + 1],
                                    woT_all[:, mi_ * 1024 + h_ * 512:mi_ * 1024 + h_ * 512 + 512],
                                    start=(mi_ == 0), stop=(mi_ == 7)),
                           inc="s_t" if h == 1 else None)
        t_done = P.cur("s_t")
        for gi, (di, h) in enumerate(sweep):
            P.emit("vector",
                   lambda e, b=fbanks[gi], di_=di, h_=h:
                   e.tensor_copy(Dt[:, di_ * 1024 + h_ * 512: di_ * 1024 + h_ * 512 + 512], pbank(b)),
                   waits=[("s_t", t_done)] if gi == 0 else [],
                   inc="s_v" if gi == len(sweep) - 1 else None)
        if sw_i < 2:
            v_done = P.cur("s_v")
            P.emit("tensor", lambda e: e.nop(), waits=[("s_v", v_done)])
    v_done = P.cur("s_v")
    sdma(wvo_scr, Dt[:], waits=[("s_v", v_done)], sem="s_scr")

    t_done = P.cur("s_t")
    for h in range(2):
        P.emit("vector",
               lambda e, h_=h: e.tensor_tensor(bvo_row[:, h_ * 512:h_ * 512 + 512],
                                               pbank(2 + h_)[0:1, :], bvo_row[:, h_ * 512:h_ * 512 + 512], ALU.add),
               waits=[("s_t", t_done), ("s_xc", n_sync_setup)] if h == 0 else [],
               inc="s_v" if h == 1 else None)
    v_done = P.cur("s_v")
    for h in range(2):
        P.emit("tensor",
               lambda e, h_=h: e.matmul(pbank(2 + h_), ones_rowf[:],
                                        bvo_row[:, h_ * 512:h_ * 512 + 512], start=True, stop=True),
               waits=[("s_v", v_done)] if h == 0 else [],
               inc="s_t" if h == 1 else None)
    t_done = P.cur("s_t")
    for h in range(2):
        P.emit("vector",
               lambda e, h_=h: e.tensor_copy(bvo_bc[:, h_ * 512:h_ * 512 + 512], pbank(2 + h_)),
               waits=[("s_t", t_done)] if h == 0 else [],
               inc="s_v" if h == 1 else None)
    P.emit("vector", lambda e: e.memset(eps_col[:], 1e-5), inc="s_v")

    az_guard = [P.cur("s_v")] * 4
    bz_guard = [0, 0, 0, 0]
    cz_guard = 0
    tanh_guard = 0
    bguard = {k: ("s_v", P.cur("s_v")) for k in range(8)}

    for b in range(B_CORE):
        batch_t = P.cur("s_t")
        if b == 1:
            gdma(Dt[:], wvo_scr, waits=[("s_t", batch_t), ("s_scr", 16)], sem="s_ldw")
        stage_guard = {0: P.cur("s_t"), 1: P.cur("s_t")}
        m_marks = {}

        xT_r = xT[b]
        for qb in range(8):
            stage = Cz if qb % 2 == 0 else D2
            psem = "s_lde" if qb % 2 == 0 else "s_ldo"
            gdma(stage.rearrange("p (a c) -> p a c", a=8),
                 xT_r[:, qb * 512:(qb + 1) * 512].rearrange("(a p) c -> p a c", p=128),
                 waits=[("s_t", stage_guard[qb % 2])], sem=psem)
            ld_need = (psem, P.cur(psem))
            for di in range(8):
                P.emit("tensor",
                       lambda e, di_=di, st=stage: e.matmul(
                           pbank(2)[0:3, :], w3dT_sb[:, di_ * 3:di_ * 3 + 3],
                           st[:, di_ * 512:di_ * 512 + 512], start=(di_ == 0), stop=(di_ == 7)),
                       waits=[ld_need, ("s_ldw", P.cur("s_ldw")), bguard[2]] if di == 0 else [],
                       inc="s_t" if di == 7 else None)
            t_done = P.cur("s_t")
            cols = slice(qb * 512, (qb + 1) * 512)
            P.emit("vector", lambda e, c=cols: e.tensor_scalar(
                aug5a[0:3, c], pbank(2)[0:3, :], b3d_col[:, 0:1], None, ALU.add),
                waits=[("s_t", t_done)], inc="s_v")
            v_pos = P.cur("s_v")
            P.emit("vector", lambda e, c=cols: e.tensor_copy(aug5b[0:3, c], aug5a[0:3, c]),
                   waits=[("s_v", v_pos)])
            P.emit("vector", lambda e, c=cols: e.tensor_tensor(possq, aug5a[0:3, c], aug5a[0:3, c], ALU.mult),
                   inc="s_v")
            v_sq = P.cur("s_v")
            bguard[2] = ("s_v", v_sq)
            P.emit("tensor", lambda e: e.matmul(
                pbank(2)[0:1, :], ones3[:], possq, start=True, stop=True),
                waits=[("s_v", v_sq)], inc="s_t")
            t_done = P.cur("s_t")
            P.emit("vector", lambda e: e.tensor_scalar_mul(mrow_st, pbank(2)[0:1, :], -0.5),
                   waits=[("s_t", t_done), ("s_m", P.cur("s_m"))], inc="s_v")
            v_mr = P.cur("s_v")
            bguard[2] = ("s_v", v_mr)
            gdma(aug5a[3:4, cols], mrow_st, waits=[("s_v", v_mr)], sem="s_m")
            gdma(aug5b[4:5, cols], mrow_st, sem="s_m")
            m_marks[qb] = P.cur("s_m")
            for qtl in range(4):
                qt = qb * 4 + qtl
                ub = (0, 1) if qtl % 2 == 0 else (3, 4)
                w0 = [bguard[ub[0]], bguard[ub[1]]]
                for di in range(8):
                    for h in range(2):
                        P.emit("tensor",
                               lambda e, di_=di, h_=h, st=stage, qtl_=qtl, b_=ub: e.matmul(
                                   pbank(b_[h_]), st[:, di_ * 512 + qtl_ * 128:di_ * 512 + qtl_ * 128 + 128],
                                   Dt[:, di_ * 1024 + h_ * 512:di_ * 1024 + h_ * 512 + 512],
                                   start=(di_ == 0), stop=(di_ == 7)),
                               waits=w0 if (di == 0 and h == 0) else [],
                               inc="s_t" if (di == 7 and h == 1) else None)
                t_done = P.cur("s_t")
                for h in range(2):
                    P.emit("vector",
                           lambda e, h_=h, qt_=qt, b_=ub: e.tensor_tensor(
                               u_b[:, qt_ * 1024 + h_ * 512:qt_ * 1024 + h_ * 512 + 512],
                               pbank(b_[h_]), bvo_bc[:, h_ * 512:h_ * 512 + 512], ALU.add),
                           waits=[("s_t", t_done)] if h == 0 else [],
                           inc="s_v" if h == 1 else None)
                bguard[ub[0]] = ("s_v", P.cur("s_v"))
                bguard[ub[1]] = ("s_v", P.cur("s_v"))
            stage_guard[qb % 2] = P.cur("s_t")

        rep_v = P.cur("s_v")
        rep_m = P.cur("s_m")
        for s4 in range(1, 4):
            gdma(aug5a[32 * s4:32 * s4 + 4, :], aug5a[0:4, :],
                 waits=[("s_v", rep_v), ("s_m", rep_m)] if s4 == 1 else [], sem="s_m")
            gdma(aug5b[32 * s4:32 * s4 + 3, :], aug5b[0:3, :], sem="s_m")
            gdma(aug5b[32 * s4 + 4:32 * s4 + 5, :], aug5b[4:5, :], sem="s_m")
        rep_done = P.cur("s_m")
        u_v_done = P.cur("s_v")
        zstt_inc = {}
        zln_inc = {}
        rc_inc = {}
        b_ready = {}
        r_ready = {0: []}
        x_marks = {}
        czb_guard = [P.cur("s_t"), P.cur("s_t")]
        czb_inc = [0, 0]

        def prefetch_x(eb):
            for qtl in range(2):
                qt = 2 * eb + qtl
                tq = qt % TQ
                sdma(Az[:, tq * 1024:tq * 1024 + 1024], x_nat[b, qt * 128:(qt + 1) * 128, :],
                     waits=[("s_v", az_guard[tq])], sem=f"s_xa{tq}")
                x_marks[qt] = (f"s_xa{tq}", P.cur(f"s_xa{tq}"))

        def emit_eprod_mms(bnd_eb, rest_eb, inject=None):
            inject = dict(inject or {})
            n_relu = 0

            def relu_done():
                nonlocal n_relu
                for fn in inject.pop(n_relu, []):
                    fn()
                n_relu += 1

            st = {"bnd_eb": bnd_eb, "rest_eb": rest_eb, "ev": {}}
            if bnd_eb is not None:
                nkt = 2 * (bnd_eb + 1)
                qcols = slice(bnd_eb * EB, (bnd_eb + 1) * EB)
                slot = (bnd_eb % 2) * 2 * EB
                need_blk = max((nkt - 1) // 4, bnd_eb // 2)
                ev = st["ev"]
                P.emit("tensor", lambda e, kt_=nkt - 2, qc=qcols: e.matmul(
                    pbank(1)[:, 0:EB], aug5a[0:5, kt_ * 128:kt_ * 128 + 128],
                    aug5b[0:5, qc],
                    start=True, stop=False),
                    waits=[bguard[1], ("s_m", m_marks[need_blk]), ("s_m", rep_done),
                           ("s_v", u_v_done)])
                P.emit("tensor", lambda e, kt_=nkt - 1, qc=qcols: e.matmul(
                    pbank(1)[:, EB:2 * EB], aug5a[0:5, kt_ * 128:kt_ * 128 + 128],
                    aug5b[0:5, qc],
                    start=False, stop=True),
                    inc="s_t")
                t_done = P.cur("s_t")
                P.emit("vector", lambda e, sl=slot: e.tensor_scalar(
                    Dbnd[:, sl:sl + 2 * EB], pbank(1)[:, 0:2 * EB], -2.0, 0.0, ALU.mult, ALU.max),
                    waits=[("s_t", t_done)], inc="s_v")
                ev[0] = ev[1] = P.cur("s_v")
                bguard[1] = ("s_v", ev[1])
                relu_done()
                relu_done()
            if rest_eb is not None and rest_eb > 0:
                nkt = 2 * (rest_eb + 1)
                nrest = nkt - 2
                qcols = slice(rest_eb * EB, (rest_eb + 1) * EB)
                banks4 = [3, 4, 2, 1]
                evac_marks = {}
                for p in range((nrest + 1) // 2):
                    kt0 = 2 * p
                    two = kt0 + 1 < nrest
                    bank = banks4[p % 4]
                    w = [bguard[bank]]
                    P.emit("tensor", lambda e, kt_=kt0, bk=bank, qc=qcols, tw=two: e.matmul(
                        pbank(bk)[:, 0:EB], aug5a[0:5, kt_ * 128:kt_ * 128 + 128],
                        aug5b[0:5, qc],
                        start=True, stop=not tw),
                        waits=w, inc=None if two else "s_t")
                    if two:
                        P.emit("tensor", lambda e, kt_=kt0 + 1, bk=bank, qc=qcols: e.matmul(
                            pbank(bk)[:, EB:2 * EB], aug5a[0:5, kt_ * 128:kt_ * 128 + 128],
                            aug5b[0:5, qc],
                            start=False, stop=True),
                            inc="s_t")
                    t_done = P.cur("s_t")
                    wid = 2 * EB if two else EB
                    P.emit("vector", lambda e, kt_=kt0, bk=bank, w_=wid: e.tensor_scalar(
                        Dt[:, kt_ * EB:kt_ * EB + w_], pbank(bk)[:, 0:w_], -2.0, 0.0, ALU.mult, ALU.max),
                        waits=[("s_t", t_done)], inc="s_v")
                    mark = P.cur("s_v")
                    evac_marks[kt0] = mark
                    if two:
                        evac_marks[kt0 + 1] = mark
                    bguard[bank] = ("s_v", mark)
                    relu_done()
                    if two:
                        relu_done()
                st["evac_marks"] = evac_marks
                st["nrest"] = nrest
            for k in sorted(inject):
                for fn in inject[k]:
                    fn()
            return st

        def emit_eprod_scal(st):
            bnd_eb, rest_eb = st["bnd_eb"], st["rest_eb"]
            maskv = None
            if bnd_eb is not None:
                slot = (bnd_eb % 2) * 2 * EB
                ev = st["ev"]
                P.emit("scalar", lambda e, sl=slot: e.activation(
                    Dbnd[:, sl:sl + 2 * EB], Dbnd[:, sl:sl + 2 * EB], AF.Sqrt, scale=1.0 / 64.0),
                    waits=[("s_v", ev[1])], inc="s_s")
                sqrtB = P.cur("s_s")
                P.emit("vector", lambda e, sl=slot: e.tensor_tensor(
                    Dbnd[:, sl:sl + EB], Dbnd[:, sl:sl + EB], mask0[:], ALU.add),
                    waits=[("s_s", sqrtB)])
                P.emit("vector", lambda e, sl=slot: e.tensor_tensor(
                    Dbnd[:, sl + EB:sl + 2 * EB], Dbnd[:, sl + EB:sl + 2 * EB], mask1[:], ALU.add),
                    inc="s_v")
                maskv = P.cur("s_v")
            sq_marks = []
            if rest_eb is not None and rest_eb > 0:
                nrest = st["nrest"]
                evac_marks = st["evac_marks"]
                cuts = sorted({2, max(3, nrest // 4), nrest // 2, nrest}) if nrest > 4 else [nrest]
                lo = 0
                for hi in cuts:
                    P.emit("scalar", lambda e, l=lo, h=hi: e.activation(
                        Dt[:, l * EB:h * EB], Dt[:, l * EB:h * EB], AF.Sqrt, scale=1.0 / 64.0),
                        waits=[("s_v", evac_marks[hi - 1])], inc="s_s")
                    sq_marks.append((lo, hi, P.cur("s_s")))
                    lo = hi
            if rest_eb is not None:
                exp_marks = []
                for (lo, hi, sqm) in sq_marks:
                    P.emit("scalar", lambda e, l=lo, h=hi: e.activation(
                        Dt[:, l * EB:h * EB], Dt[:, l * EB:h * EB], AF.Exp, scale=-1.0),
                        waits=[("s_s", sqm)], inc="s_s")
                    exp_marks.append((lo, hi, P.cur("s_s")))
                r_ready[rest_eb] = exp_marks
            if bnd_eb is not None:
                slot = (bnd_eb % 2) * 2 * EB
                P.emit("scalar", lambda e, sl=slot: e.activation(
                    Dbnd[:, sl:sl + 2 * EB], Dbnd[:, sl:sl + 2 * EB], AF.Exp, scale=-1.0),
                    waits=[("s_v", maskv)], inc="s_s")
                b_ready[bnd_eb] = P.cur("s_s")

        def emit_attn_q(eb, qtl):
            nkt = 2 * (eb + 1)
            slot = (eb % 2) * 2 * EB
            exp_marks = r_ready[eb]
            if True:
                qt = 2 * eb + qtl
                tq = qt % TQ
                yb = (0, 1) if qtl == 0 else (3, 4)
                yfull = ps[:, 512 * yb[0]:512 * yb[0] + 1024]
                x_need = x_marks[qt]
                nb = 1 if qtl == 0 else 2
                kts = list(range(nb)) + list(range(nkt - 2))
                first_waits = [("s_s", b_ready[eb]), bguard[yb[0]], bguard[yb[1]]]
                r_wait = [bguard[2]] if qtl == 0 else []
                rcol = 1024 + qtl
                for i in range(len(kts)):
                    isb = i < nb
                    kt = kts[i]
                    gkt = (nkt - 2 + kt) if isb else kt
                    if isb:
                        lhs = Dbnd[:, slot + kt * EB + qtl * 128:slot + kt * EB + qtl * 128 + 128]
                    else:
                        lhs = Dt[:, kt * EB + qtl * 128:kt * EB + qtl * 128 + 128]
                    w = first_waits if i == 0 else []
                    if not isb:
                        for (lo, hi, em) in exp_marks:
                            if kt == lo:
                                w = w + [("s_s", em)]
                    P.emit("tensor", lambda e, l=lhs, kt_=gkt, fst=(i == 0), lst=(i == len(kts) - 1), yb_=yb: e.matmul(
                        pbank(yb_[0]), l, u_b[:, kt_ * 1024:kt_ * 1024 + 512],
                        start=fst, stop=lst),
                        waits=w)
                    P.emit("tensor", lambda e, l=lhs, kt_=gkt, fst=(i == 0), lst=(i == len(kts) - 1), yb_=yb: e.matmul(
                        pbank(yb_[1]), l, u_b[:, kt_ * 1024 + 512:kt_ * 1024 + 1024],
                        start=fst, stop=lst))
                    P.emit("tensor", lambda e, l=lhs, fst=(i == 0), lst=(i == len(kts) - 1),
                           rc=rcol, q_=qtl: e.matmul(
                        ps[:, rc:rc + 1], l, ones_col[:],
                        start=(fst and q_ == 0), stop=lst),
                        waits=r_wait if i == 0 else [],
                        inc="s_t" if i == len(kts) - 1 else None)
                t_attn = P.cur("s_t")
                st_sl = stat[:, tq * 4:tq * 4 + 4]
                P.emit("vector", lambda e, sl=st_sl, rc=rcol: e.tensor_copy(sl[:, 3:4], ps[:, rc:rc + 1]),
                       waits=[("s_t", t_attn)], inc="s_v")
                rc_inc[qt] = P.cur("s_v")
                bguard[2] = ("s_v", rc_inc[qt])
                P.emit("vector", lambda e, sl=st_sl: e.reciprocal(sl[:, 0:1], sl[:, 3:4]),
                       waits=[("s_v", rc_inc[qt])], inc="s_v")
                v_rec = P.cur("s_v")
                P.emit("vector", lambda e, sl=st_sl, tq_=tq, yf=yfull: e.scalar_tensor_tensor(
                    Az[:, tq_ * 1024:tq_ * 1024 + 1024], yf, sl[:, 0:1],
                    Az[:, tq_ * 1024:tq_ * 1024 + 1024], ALU.mult, ALU.add,
                    accum_out=sl[:, 1:2]),
                    waits=[("s_v", v_rec), x_need], inc="s_v")
                zstt_inc[qt] = P.cur("s_v")
                bguard[yb[0]] = ("s_v", zstt_inc[qt])
                bguard[yb[1]] = ("s_v", zstt_inc[qt])

        ln_sq_mark = {}
        ln_var_mark = {}
        ln_std_mark = {}

        def ln_sq(eb, qtl):
            qt = 2 * eb + qtl
            tq = qt % TQ
            st_sl = stat[:, tq * 4:tq * 4 + 4]
            P.emit("scalar", lambda e, sl=st_sl, tq_=tq: e.activation(
                Bz[:, tq_ * 1024:tq_ * 1024 + 1024], Az[:, tq_ * 1024:tq_ * 1024 + 1024],
                AF.Square, accum_out=sl[:, 2:3]),
                waits=[("s_v", zstt_inc[qt]), (f"s_ob{tq}", bz_guard[tq])], inc="s_s")
            ln_sq_mark[qt] = P.cur("s_s")

        def ln_stats(eb, qtl):
            qt = 2 * eb + qtl
            tq = qt % TQ
            st_sl = stat[:, tq * 4:tq * 4 + 4]
            P.emit("vector", lambda e, sl=st_sl: e.tensor_scalar_mul(sl[:, 0:1], sl[:, 1:2], 1.0 / Dm),
                   waits=[("s_s", ln_sq_mark[qt])], inc="s_v")
            v_mu = P.cur("s_v")
            P.emit("vector", lambda e, sl=st_sl: e.tensor_tensor(sl[:, 1:2], sl[:, 0:1], sl[:, 0:1], ALU.mult),
                   waits=[("s_v", v_mu)], inc="s_v")
            v_mu2 = P.cur("s_v")
            P.emit("vector", lambda e, sl=st_sl: e.scalar_tensor_tensor(
                sl[:, 2:3], sl[:, 2:3], 1.0 / Dm, sl[:, 1:2], ALU.mult, ALU.subtract),
                waits=[("s_v", v_mu2)], inc="s_v")
            ln_var_mark[qt] = P.cur("s_v")

        def ln_sqrt(eb, qtl):
            qt = 2 * eb + qtl
            tq = qt % TQ
            st_sl = stat[:, tq * 4:tq * 4 + 4]
            P.emit("scalar", lambda e, sl=st_sl: e.activation(sl[:, 3:4], sl[:, 2:3], AF.Sqrt, bias=eps_col[:, 0:1]),
                   waits=[("s_v", ln_var_mark[qt])], inc="s_s")
            ln_std_mark[qt] = P.cur("s_s")

        def ln_norm(eb, qtl):
            qt = 2 * eb + qtl
            tq = qt % TQ
            st_sl = stat[:, tq * 4:tq * 4 + 4]
            P.emit("vector", lambda e, sl=st_sl: e.reciprocal(sl[:, 1:2], sl[:, 3:4]),
                   waits=[("s_s", ln_std_mark[qt])], inc="s_v")
            v_rstd = P.cur("s_v")
            P.emit("vector", lambda e, sl=st_sl, tq_=tq, q_=qtl: e.tensor_scalar(
                CZB[:, q_ * 1024:q_ * 1024 + 1024], Az[:, tq_ * 1024:tq_ * 1024 + 1024],
                sl[:, 0:1], sl[:, 1:2], ALU.subtract, ALU.mult),
                waits=[("s_v", v_rstd), ("s_t", czb_guard[qtl])], inc="s_v")
            czb_inc[qtl] = P.cur("s_v")

        def ln_norm_bz(eb, qtl):
            qt = 2 * eb + qtl
            tq = qt % TQ
            st_sl = stat[:, tq * 4:tq * 4 + 4]
            P.emit("vector", lambda e, sl=st_sl, tq_=tq: e.tensor_scalar(
                Bz[:, tq_ * 1024:tq_ * 1024 + 1024], Az[:, tq_ * 1024:tq_ * 1024 + 1024],
                sl[:, 0:1], sl[:, 1:2], ALU.subtract, ALU.mult),
                inc="s_v")
            zln_inc[qt] = P.cur("s_v")

        def emit_tail_pre(tb, qs_list):
            for idx, qtl in enumerate(qs_list):
                src_half = idx
                for ct in range(8):
                    P.emit("tensor", lambda e, ct_=ct, sh=src_half: e.transpose(
                        ps_tr[:, ct_ * 128:ct_ * 128 + 128],
                        CZB[:, sh * 1024 + ct_ * 128:sh * 1024 + ct_ * 128 + 128], ident_sb[:]),
                        waits=([("s_v", czb_inc[src_half]), bguard[5]] if ct == 0 else []),
                        inc="s_t" if ct == 7 else None)
                t_tr = P.cur("s_t")
                czb_guard[src_half] = t_tr
                P.emit("vector", lambda e, qs_=qtl: e.tensor_copy(
                    D2.rearrange("p (a q) -> p a q", a=8)[:, :, qs_ * 128:qs_ * 128 + 128],
                    ps_tr.rearrange("p (a q) -> p a q", a=8)),
                    waits=[("s_t", t_tr)], inc="s_v")
                bguard[5] = ("s_v", P.cur("s_v"))

        def emit_tail(tb):
            nonlocal cz_guard, tanh_guard
            v_start = P.cur("s_v")
            for layer, (wsb, src_, dst) in enumerate([(W_sb, D2, Cz), (A1_sb, Cz, D2)]):
                v_start = P.cur("s_v")
                for eo in range(8):
                    bank = 6 + (eo % 2)
                    for c in range(8):
                        P.emit("tensor", lambda e, w=wsb, s=src_, c_=c, eo_=eo, bk=bank: e.matmul(
                            pbank(bk), w[:, c_ * 1024 + eo_ * 128:c_ * 1024 + eo_ * 128 + 128],
                            s[:, c_ * 512:c_ * 512 + 512], start=(c_ == 0), stop=(c_ == 7)),
                            waits=([("s_v", v_start), bguard[bank], ("s_s", tanh_guard)]
                                   if c == 0 else []),
                            inc="s_t" if c == 7 else None)
                    t_h = P.cur("s_t")
                    P.emit("vector", lambda e, d=dst, eo_=eo, bk=bank: e.tensor_copy(
                        d[:, eo_ * 512:eo_ * 512 + 512], pbank(bk)),
                        waits=[("s_t", t_h)], inc="s_v")
                    bguard[bank] = ("s_v", P.cur("s_v"))
            v_h2 = P.cur("s_v")
            s_f = {}
            for qtl in range(TQ):
                hb = (0, 1) if qtl % 2 == 0 else (6, 7)
                for e2 in range(8):
                    for h in range(2):
                        P.emit("tensor", lambda e, e2_=e2, h_=h, q_=qtl, b_=hb: e.matmul(
                            pbank(b_[h_]), D2[:, e2_ * 512 + q_ * 128:e2_ * 512 + q_ * 128 + 128],
                            A2_sb[:, e2_ * 1024 + h_ * 512:e2_ * 1024 + h_ * 512 + 512],
                            start=(e2_ == 0), stop=False),
                            waits=([("s_v", v_h2), bguard[hb[0]], bguard[hb[1]]]
                                   if (e2 == 0 and h == 0) else []))
                for h in range(2):
                    P.emit("tensor", lambda e, h_=h, b_=hb: e.matmul(
                        pbank(b_[h_]), ones_row[:], bias_row[:, h_ * 512:h_ * 512 + 512],
                        start=False, stop=True),
                        inc="s_t" if h == 1 else None)
                t_h3 = P.cur("s_t")
                for h in range(2):
                    P.emit("scalar", lambda e, h_=h, tq_=qtl, b_=hb: e.activation(
                        Az[:, tq_ * 1024 + h_ * 512:tq_ * 1024 + h_ * 512 + 512], pbank(b_[h_]), AF.Tanh),
                        waits=[("s_t", t_h3)] if h == 0 else [],
                        inc="s_s" if h == 1 else None)
                s_f[qtl] = P.cur("s_s")
                tanh_guard = s_f[qtl]
                bguard[hb[0]] = ("s_s", tanh_guard)
                bguard[hb[1]] = ("s_s", tanh_guard)
            s_sq = {}
            for qtl in range(TQ):
                st_sl = stat[:, (8 + qtl) * 4:(8 + qtl) * 4 + 4]
                P.emit("vector", lambda e, sl=st_sl, tq_=qtl: e.scalar_tensor_tensor(
                    Az[:, tq_ * 1024:tq_ * 1024 + 1024], Az[:, tq_ * 1024:tq_ * 1024 + 1024], 0.0,
                    Bz[:, tq_ * 1024:tq_ * 1024 + 1024], ALU.add, ALU.add, accum_out=sl[:, 0:1]),
                    waits=[("s_s", s_f[qtl])], inc="s_v")
                v_z2 = P.cur("s_v")
                P.emit("scalar", lambda e, sl=st_sl, tq_=qtl: e.activation(
                    Bz[:, tq_ * 1024:tq_ * 1024 + 1024], Az[:, tq_ * 1024:tq_ * 1024 + 1024],
                    AF.Square, accum_out=sl[:, 2:3]),
                    waits=[("s_v", v_z2)], inc="s_s")
                s_sq[qtl] = P.cur("s_s")
            v_var = {}
            for qtl in range(TQ):
                st_sl = stat[:, (8 + qtl) * 4:(8 + qtl) * 4 + 4]
                P.emit("vector", lambda e, sl=st_sl: e.tensor_scalar_mul(sl[:, 0:1], sl[:, 0:1], 1.0 / Dm),
                       waits=[("s_s", s_sq[qtl])], inc="s_v")
                v_mu = P.cur("s_v")
                P.emit("vector", lambda e, sl=st_sl: e.tensor_tensor(sl[:, 1:2], sl[:, 0:1], sl[:, 0:1], ALU.mult),
                       waits=[("s_v", v_mu)], inc="s_v")
                v_mu2 = P.cur("s_v")
                P.emit("vector", lambda e, sl=st_sl: e.scalar_tensor_tensor(
                    sl[:, 2:3], sl[:, 2:3], 1.0 / Dm, sl[:, 1:2], ALU.mult, ALU.subtract),
                    waits=[("s_v", v_mu2)], inc="s_v")
                v_var[qtl] = P.cur("s_v")
            s_std = {}
            for qtl in range(TQ):
                st_sl = stat[:, (8 + qtl) * 4:(8 + qtl) * 4 + 4]
                P.emit("scalar", lambda e, sl=st_sl: e.activation(sl[:, 3:4], sl[:, 2:3], AF.Sqrt, bias=eps_col[:, 0:1]),
                       waits=[("s_v", v_var[qtl])], inc="s_s")
                s_std[qtl] = P.cur("s_s")
            for qtl in range(TQ):
                qt = tb * TQ + qtl
                st_sl = stat[:, (8 + qtl) * 4:(8 + qtl) * 4 + 4]
                P.emit("vector", lambda e, sl=st_sl: e.reciprocal(sl[:, 1:2], sl[:, 3:4]),
                       waits=[("s_s", s_std[qtl])], inc="s_v")
                v_rstd = P.cur("s_v")
                P.emit("vector", lambda e, sl=st_sl, tq_=qtl: e.tensor_scalar(
                    Bz[:, tq_ * 1024:tq_ * 1024 + 1024], Az[:, tq_ * 1024:tq_ * 1024 + 1024],
                    sl[:, 0:1], sl[:, 1:2], ALU.subtract, ALU.mult),
                    waits=[("s_v", v_rstd)], inc="s_v")
                v_out = P.cur("s_v")
                az_guard[qtl] = v_out
                sdma(out_d[b, qt * 128:(qt + 1) * 128, :], Bz[:, qtl * 1024:qtl * 1024 + 1024],
                     waits=[("s_v", v_out)], sem=f"s_ob{qtl}")
                bz_guard[qtl] = P.cur(f"s_ob{qtl}")
            cz_guard = P.cur("s_t")

        prefetch_x(0)
        emit_eprod_scal(emit_eprod_mms(0, None))
        emit_eprod_scal(emit_eprod_mms(1, 0))
        prefetch_x(1)
        for eb in range(NEB):
            emit_attn_q(eb, 0)
            P.emit("scalar", lambda e: e.activation(
                stat[:, 48:49], eps_col[:, 0:1], AF.Sqrt))
            ln_sq(eb, 0)
            emit_attn_q(eb, 1)
            ln_sq(eb, 1)
            inj = {9: [lambda e_=eb: ln_stats(e_, 0), lambda e_=eb: ln_stats(e_, 1),
                       lambda e_=eb: ln_sqrt(e_, 0), lambda e_=eb: ln_sqrt(e_, 1)],
                   11: [lambda e_=eb: ln_norm(e_, 0)],
                   13: [lambda e_=eb: ln_norm(e_, 1)],
                   98: [lambda e_=eb: ln_norm_bz(e_, 0)],
                   99: [lambda e_=eb: ln_norm_bz(e_, 1)]}
            ep_st = emit_eprod_mms(eb + 2 if eb + 2 < NEB else None,
                                   eb + 1 if eb + 1 < NEB else None,
                                   inject=inj)
            emit_eprod_scal(ep_st)
            emit_tail_pre(eb // 2, [0, 1] if eb % 2 == 0 else [2, 3])
            if eb % 2 == 1:
                emit_tail(eb // 2)
                if eb + 1 < NEB:
                    prefetch_x(eb + 1)
                if eb + 2 < NEB:
                    prefetch_x(eb + 2)

    P.emit("sync", lambda e: e.nop(),
           waits=[(f"s_ob{i}", P.cur(f"s_ob{i}")) for i in range(4)])
    P.check_deadlock()

    with nc.Block() as block:
        def run(eng_obj, lst):
            for waits, fn, inc, inc_n in lst:
                for s, v in waits:
                    eng_obj.wait_ge(sems[s], v)
                inst = fn(eng_obj)
                if inc is not None:
                    inst.then_inc(sems[inc], inc_n)

        @block.tensor
        def _(tensor):
            run(tensor, P.lists["tensor"])

        @block.vector
        def _(vector):
            run(vector, P.lists["vector"])

        @block.scalar
        def _(scalar):
            run(scalar, P.lists["scalar"])

        @block.gpsimd
        def _(gpsimd):
            run(gpsimd, P.lists["gpsimd"])

        @block.sync
        def _(sync):
            run(sync, P.lists["sync"])

    es.close()
    return nc


_NC_CACHE = None


def kernel(L0_attn_x, L0_attn_wq, L0_attn_bq, L0_attn_wk, L0_attn_bk, L0_attn_wv,
           L0_attn_bv, L0_attn_wo, L0_attn_bo, L0_attn_w3d, L0_attn_b3d,
           L0_ffn_weight, L0_ffn_aux_weight1, L0_ffn_aux_weight2, L0_ffn_bias):
    global _NC_CACHE
    x = np.ascontiguousarray(np.asarray(L0_attn_x, dtype=np.float32))
    wv = np.ascontiguousarray(np.asarray(L0_attn_wv, dtype=np.float32))
    woT = np.ascontiguousarray(np.asarray(L0_attn_wo, dtype=np.float32).T)
    bv = np.ascontiguousarray(np.asarray(L0_attn_bv, dtype=np.float32))
    bo = np.ascontiguousarray(np.asarray(L0_attn_bo, dtype=np.float32))
    w3d = np.ascontiguousarray(np.asarray(L0_attn_w3d, dtype=np.float32))
    b3d = np.ascontiguousarray(np.asarray(L0_attn_b3d, dtype=np.float32))
    fW = np.ascontiguousarray(np.asarray(L0_ffn_weight, dtype=np.float32))
    fA1 = np.ascontiguousarray(np.asarray(L0_ffn_aux_weight1, dtype=np.float32))
    fA2 = np.ascontiguousarray(np.asarray(L0_ffn_aux_weight2, dtype=np.float32))
    fbias = np.ascontiguousarray(np.asarray(L0_ffn_bias, dtype=np.float32).reshape(1, Dm))

    w3dT = np.ascontiguousarray(w3d.T.reshape(8, 128, 3))
    bvT = np.ascontiguousarray(bv.reshape(8, 128).T)
    bo_row = np.ascontiguousarray(bo.reshape(1, Dm))
    b3d_col = np.ascontiguousarray(b3d.reshape(3, 1))
    ident = np.eye(128, dtype=np.float32)
    ones_in = np.ones((128, 128), dtype=np.float32)
    f = np.arange(EB)[None, :]
    p = np.arange(128)[:, None]
    mask0 = np.where(f >= p, 0.0, 100.0).astype(np.float32)
    mask1 = np.where(f - p >= 128, 0.0, 100.0).astype(np.float32)

    if _NC_CACHE is None:
        _NC_CACHE = build_nc()
    nc = _NC_CACHE

    in_maps = []
    for c in range(8):
        xc = x[2 * c:2 * c + 2]
        in_maps.append({
            "x_nat": np.ascontiguousarray(xc),
            "xT": np.ascontiguousarray(xc.transpose(0, 2, 1)),
            "wv": wv, "woT": woT, "w3dT": w3dT, "bvT": bvT,
            "bo_row": bo_row, "b3d_col": b3d_col,
            "fW": fW, "fA1": fA1, "fA2": fA2, "fbias": fbias,
            "ident": ident, "mask0": mask0, "mask1": mask1, "ones_in": ones_in,
            "ones3_in": np.ones((3, 1), dtype=np.float32),
            "onescol_in": np.ones((128, 1), dtype=np.float32),
            "onesrow4k": np.ones((1, S), dtype=np.float32),
        })
    res = run_bass_kernel_spmd(nc, in_maps, core_ids=list(range(8)))
    out = np.concatenate([np.asarray(r["out"], dtype=np.float32) for r in res.results], axis=0)
    return out.reshape(16, S, Dm)
